# version 13
# baseline (speedup 1.0000x reference)
import contextlib
import os

import numpy as np
import ml_dtypes

import concourse.bass as bass
import concourse.mybir as mybir
from concourse.bass_utils import run_bass_kernel_spmd
from concourse.alu_op_type import AluOpType as OP

F32 = mybir.dt.float32
BF16 = mybir.dt.bfloat16
I16 = mybir.dt.int16
I32 = mybir.dt.int32
U16 = mybir.dt.uint16
AF = mybir.ActivationFunctionType
AX = mybir.AxisListType.X
BF = ml_dtypes.bfloat16

L = 2048
D = 1024
H = 4
SEG = 256
NSEG = L // SEG
NTT = SEG // 128
NTILE = L // 128
DI = 3072
NOC = DI // 128
OSH = DI // 8
AGW = 2 * D * OSH + DI * 128

_ENGS = ("sync", "tensor", "vector", "scalar", "gpsimd")


class Sem:
    __slots__ = ("h", "n", "name")

    def __init__(self, h, name):
        self.h, self.n, self.name = h, 0, name


class Rec:
    def __init__(self):
        self.ops = {e: [] for e in _ENGS}
        self.auto = {}
        self.hiwater = {}

    def op(self, eng, build, sig=None, sign=1, dma=False):
        auto = None if dma else self.auto.get(eng)
        if auto is not None:
            if sig is None:
                sig = auto
            assert sig is auto, f"{eng} ops must sig their own sem"
            if auto.n > 0:
                self.ops[eng].append(
                    lambda e, s=auto, vv=auto.n: e.wait_ge(s.h, vv))
        if sig is not None:
            sig.n += sign

            def wrapped(e, b=build, s=sig, v=sign):
                b(e).then_inc(s.h, v)
        else:
            def wrapped(e, b=build):
                b(e)
        self.ops[eng].append(wrapped)
        return sig.n if sig is not None else None

    def wait(self, eng, sem, val=None):
        v = sem.n if val is None else val
        if v <= 0:
            return
        key = (eng, sem.name)
        if v <= self.hiwater.get(key, 0):
            return
        self.hiwater[key] = v
        self.ops[eng].append(lambda e, s=sem, vv=v: e.wait_ge(s.h, vv))

    def flush(self, block):
        for eng in _ENGS:
            lst = self.ops[eng]
            if not lst:
                continue

            def body(e, lst=list(lst)):
                for f in lst:
                    f(e)
            getattr(block, eng)(body)
        self.ops = {e: [] for e in _ENGS}


def _masks_host():
    c = 32
    Lb = np.zeros((128, 128), np.float32)
    Ub = np.zeros((128, 128), np.float32)
    Ib = np.zeros((128, 128), np.float32)
    for p in range(4):
        s = slice(32 * p, 32 * p + 32)
        Lb[s, s] = np.tril(np.ones((c, c), np.float32), -1)
        Ub[s, s] = np.triu(np.ones((c, c), np.float32), 1)
        Ib[s, s] = np.eye(c, dtype=np.float32)
    CM = np.zeros((128, 128), np.float32)
    for i in range(4):
        for j in range(4):
            if i > j:
                CM[32 * i:32 * i + 32, 32 * j:32 * j + 32] = 1.0
    MT = np.zeros((128, 128), np.float32)
    for i in range(4):
        for j in range(4):
            bi, bj = slice(32 * i, 32 * i + 32), slice(32 * j, 32 * j + 32)
            if i > j:
                MT[bj, bi] = 1.0
            elif i == j:
                MT[bj, bi] = np.tril(np.ones((c, c), np.float32), 0).T
    pat = np.zeros((32, 1024), np.float32)
    for b in range(32):
        pat[b, 32 * b:32 * b + 32] = 1.0
    magic = np.full((128, 8), 0x5F3759DF, np.int32)
    return dict(
        m_negL=(-Lb).astype(BF), m_negU=(-Ub).astype(BF), m_I=Ib.astype(BF),
        m_cmLneg=(-CM).astype(BF), m_cmUpos=CM.T.copy().astype(BF),
        m_cmUneg=(-CM.T).astype(BF), m_MT=MT.astype(BF),
        identB=np.eye(128, dtype=BF), identF=np.eye(128, dtype=np.float32),
        pat=pat, magic=magic,
    )


def _host_prepare(inputs):
    hs = np.ascontiguousarray(inputs["L0_attn_hidden_states"], dtype=np.float32)
    WqT = np.ascontiguousarray(np.asarray(inputs["L0_attn_Wq"], np.float32).T)
    WkT = np.ascontiguousarray(np.asarray(inputs["L0_attn_Wk"], np.float32).T)
    WvT = np.ascontiguousarray(np.asarray(inputs["L0_attn_Wv"], np.float32).T)
    WoT = np.ascontiguousarray(np.asarray(inputs["L0_attn_Wo"], np.float32).T)
    WbmT = np.ascontiguousarray(np.concatenate(
        [np.asarray(inputs["L0_attn_Wb"], np.float32),
         np.asarray(inputs["L0_attn_Wmix"], np.float32)], 0).T)
    convw = np.ascontiguousarray(
        np.stack([np.asarray(inputs["L0_attn_conv_q"], np.float32),
                  np.asarray(inputs["L0_attn_conv_k"], np.float32),
                  np.asarray(inputs["L0_attn_conv_v"], np.float32)], 0)
        .reshape(3, 8, 128, 4).transpose(2, 0, 1, 3).reshape(128, 96))
    normw_t = np.ascontiguousarray(
        np.tile(np.asarray(inputs["L0_attn_norm_w"], np.float32), H)
        .reshape(8, 128).T)
    w_up = np.asarray(inputs["L0_ffn_w_up"], np.float32)
    w_gate = np.asarray(inputs["L0_ffn_w_gate"], np.float32)
    w_down = np.asarray(inputs["L0_ffn_w_down"], np.float32)
    masks = _masks_host()
    in_maps = []
    for c in range(8):
        in_maps.append(dict(
            x0=np.ascontiguousarray(hs[c]),
            wqt=WqT, wkt=WkT, wvt=WvT, wot=WoT, wbmt=WbmT,
            convw=convw, normwt=normw_t,
            upsh=np.ascontiguousarray(w_up[OSH * c:OSH * (c + 1)]),
            gash=np.ascontiguousarray(w_gate[OSH * c:OSH * (c + 1)]),
            dnsh=np.ascontiguousarray(w_down[128 * c:128 * (c + 1)]),
            **masks,
        ))
    return in_maps


class Carver:

    def __init__(self, arena_ap, ncols):
        self.a = arena_ap
        self.n = ncols
        self.off = 0

    def take(self, cols, dtype=BF16):
        o = self.off
        self.off += cols
        assert self.off <= self.n, f"arena overflow {self.off} > {self.n}"
        v = self.a[:, o:o + cols]
        if dtype is not BF16:
            v = v.bitcast(dtype)
        return v

    def at(self, off, cols, dtype=BF16):
        v = self.a[:, off:off + cols]
        if dtype is not BF16:
            v = v.bitcast(dtype)
        return v


def _build():
    nc = bass.Bass("TRN2", target_bir_lowering=False)
    es = contextlib.ExitStack()

    dt_in = lambda n, s, d: nc.dram_tensor(n, s, d, kind="ExternalInput").ap()
    d_x0 = dt_in("x0", [L, D], F32)
    d_wqt = dt_in("wqt", [D, D], F32)
    d_wkt = dt_in("wkt", [D, D], F32)
    d_wvt = dt_in("wvt", [D, D], F32)
    d_wot = dt_in("wot", [D, D], F32)
    d_wbmt = dt_in("wbmt", [D, 8], F32)
    d_convw = dt_in("convw", [128, 96], F32)
    d_normwt = dt_in("normwt", [128, 8], F32)
    d_upsh = dt_in("upsh", [OSH, D], F32)
    d_gash = dt_in("gash", [OSH, D], F32)
    d_dnsh = dt_in("dnsh", [128, DI], F32)
    d_cst = {}
    for k in ("m_negL", "m_negU", "m_I", "m_cmLneg", "m_cmUpos", "m_cmUneg",
              "m_MT", "identB"):
        d_cst[k] = dt_in(k, [128, 128], BF16)
    d_cst["identF"] = dt_in("identF", [128, 128], F32)
    d_cst["pat"] = dt_in("pat", [32, 1024], F32)
    d_cst["magic"] = dt_in("magic", [128, 8], I32)
    d_out = nc.dram_tensor("out", [L, D], F32, kind="ExternalOutput").ap()

    _dbg = os.environ.get("BASSMK_DEBUG", "") == "1"
    d_x1sp = nc.dram_tensor("x1sp", [L, D], F32,
                            kind="ExternalOutput" if _dbg else "Internal").ap()
    d_upsp = nc.dram_tensor("upsp", [NOC, 128, L], BF16,
                            kind="ExternalOutput" if _dbg else "Internal").ap()
    d_sgu = nc.dram_tensor("sgu", [NOC, 128, L], U16).ap()
    d_agi = nc.dram_tensor("agi", [AGW], BF16).ap()
    d_ago = nc.dram_tensor("ago", [8, AGW], BF16).ap()
    d_mxi = nc.dram_tensor("mxi", [128, 1], F32).ap()
    d_mxo = nc.dram_tensor("mxo", [128, 1], F32).ap()
    d_mxi2 = nc.dram_tensor("mxi2", [2, 128], F32).ap()
    d_mxo2 = nc.dram_tensor("mxo2", [2, 128], F32).ap()
    d_sc = nc.dram_tensor("dsc", [1, 16], F32,
                          kind="ExternalOutput" if _dbg else "Internal").ap()
    d_ago_dbg = _dbg

    sb = lambda n, s, d: es.enter_context(nc.sbuf_tensor(n, s, d))

    c_idB = sb("c_idB", [128, 128], BF16)
    c_idF = sb("c_idF", [128, 128], F32)
    c_mL = sb("c_mL", [128, 128], BF16)
    c_mU = sb("c_mU", [128, 128], BF16)
    c_mI = sb("c_mI", [128, 128], BF16)
    c_cmLn = sb("c_cmLn", [128, 128], BF16)
    c_cmUp = sb("c_cmUp", [128, 128], BF16)
    c_cmUn = sb("c_cmUn", [128, 128], BF16)
    c_MT = sb("c_MT", [128, 128], BF16)
    c_convw = sb("c_convw", [128, 96], F32)
    c_normwt = sb("c_normwt", [128, 8], F32)
    c_magic = sb("c_magic", [128, 8], I32)

    AR_A = sb("AR_A", [128, 12288], BF16)
    AR_W = sb("AR_W", [128, 3 * 8192], BF16)
    AR_E = sb("AR_E", [128, 49152], BF16)
    w_oT = sb("w_oT", [128, 8192], BF16)
    w_bmT = sb("w_bmT", [128, 64], BF16)

    b_x1 = sb("b_x1", [128, 1024], F32)
    b_xr = sb("b_xr", [128, 1024], F32)
    b_acc = sb("b_acc", [128, 48], F32)
    b_rn = sb("b_rn", [128, 32], F32)
    b_bm = sb("b_bm", [128, NTT * 8], F32)
    b_bx = sb("b_bx", [128, NTT * 8], F32)
    b_ln2 = sb("b_ln2", [128, 32], F32)
    b_mxp = sb("b_mxp", [128, 20], F32)
    b_rs = sb("b_rs", [128, 16], F32)
    b_sc1 = sb("b_sc1", [2, 16], F32)
    b_row = sb("b_row", [2, 128], F32)
    b_wst = sb("b_wst", [128, 4096], BF16)
    b_i32 = sb("b_i32", [128, 8], I32)
    b_jnkV = sb("b_jnkV", [128, 32], F32)

    c_diag = AR_A[:, 0:12288]
    v_fst = AR_A[:, 0:2048]
    xqT_A = AR_A[:, 2048:12288]
    xqT_B = w_oT[:, 0:6144]
    t_prod = [AR_A[:, 4096 * p:4096 * (p + 1)].bitcast(F32) for p in range(2)]
    t_upb = [AR_A[:, 8192 + 2048 * p:8192 + 2048 * (p + 1)] for p in range(2)]
    t_sigF = [w_oT[:, 4096 * p:4096 * (p + 1)].bitcast(F32) for p in range(2)]
    t_sigU = [b_wst[:, 2048 * p:2048 * (p + 1)].bitcast(U16) for p in range(2)]
    t_oI = w_oT[:].bitcast(I16)
    t_u4 = [AR_A[:, 8192 + 512 * j:8192 + 512 * (j + 1)] for j in range(4)]
    t_p4 = [AR_A[:, 1024 * j:1024 * (j + 1)].bitcast(F32) for j in range(4)]
    x1b = [b_x1[:], b_wst[:, 0:2048].bitcast(F32)]
    xrb = [b_xr[:], b_wst[:, 2048:4096].bitcast(F32)]

    def xq_sl(it, c0, cn):
        if it < 5:
            return xqT_A[:, it * 2048 + c0:it * 2048 + c0 + cn]
        return xqT_B[:, (it - 5) * 2048 + c0:(it - 5) * 2048 + c0 + cn]

    w_qT = AR_W[:, 0:8192]
    w_kT = AR_W[:, 8192:16384]
    w_vT = AR_W[:, 16384:24576]
    w_dnT = AR_W[:]

    ce = Carver(AR_E[:], 49152)
    b_xT = ce.take(2048)
    b_praw = ce.take(3 * 8 * 260)
    b_qf = ce.take(2048)
    b_ksil = ce.take(2048)
    b_vsil = ce.take(2048)
    b_khf = ce.take(2048)
    b_kbf = ce.take(2048)
    b_ktok = ce.take(2048)
    b_kbtok = ce.take(2048)
    b_vtok = ce.take(2048)
    b_vbtok = ce.take(2048)
    b_onrm = ce.take(2048)
    b_xln = ce.take(1024)
    b_x0t = ce.take(4096, F32)
    b_o2 = ce.take(2048, F32)
    b_oT = ce.take(1024)
    b_jnkS = ce.take(2048, F32)
    b_Pp = [ce.take(256) for _ in range(H)]
    b_Qp = [ce.take(256) for _ in range(H)]
    b_Mp = [ce.take(256) for _ in range(H)]
    b_nw = [ce.take(256) for _ in range(H)]
    b_U = [ce.take(256) for _ in range(H)]
    b_Lt = [ce.take(128) for _ in range(H)]
    b_LtTp = [ce.take(128) for _ in range(H)]
    b_LtTn = [ce.take(128) for _ in range(H)]
    b_L2T = [ce.take(128) for _ in range(H)]
    b_GT = [ce.take(128) for _ in range(H)]
    b_S = [ce.take(512) for _ in range(H)]
    p0 = Carver(AR_E[:], 49152)
    b_shard = p0.take(6144, F32)
    b_shq = p0.take(3072, I16)
    b_shb = p0.take(3072)
    b_sst = p0.take(192, F32)
    b_sin = p0.take(192, F32)
    b_stg = p0.take(1024)
    b_sexp_host = p0.take(1536, F32)
    b_sexp = b_sexp_host[0:32, 0:768]
    c_pat_host = p0.take(2048, F32)
    c_pat = c_pat_host[0:32, 0:1024]
    g_slot = lambda oc: AR_E[:, 2048 * oc:2048 * (oc + 1)]

    ps = [es.enter_context(nc.psum_tensor(f"ps{i}", [128, 512], F32))
          for i in range(8)]

    def pb(i):
        return ps[i][:].bitcast(BF16)

    sm = {k: Sem(es.enter_context(nc.semaphore(k)), k) for k in
          ("t", "v", "s", "cc",
           "cst", "shard", "x00", "x01", "x1st", "x1ld", "upsp", "upld",
           "wst0", "wst1", "dnasm", "bar", "out", "wgt", "agwr", "gld",
           "gld2")}
    T, V, S, CC = (sm[k] for k in ("t", "v", "s", "cc"))
    G = Sem(es.enter_context(nc.semaphore("g")), "g")
    P_CST, P_SHARD, P_X00, P_X01 = (sm[k] for k in ("cst", "shard", "x00", "x01"))
    P_X1ST, P_X1LD, P_UPSP, P_UPLD = (sm[k] for k in ("x1st", "x1ld", "upsp", "upld"))
    P_WST = (sm["wst0"], sm["wst1"])
    P_DN, P_BAR, P_OUT, P_WGT, P_AGWR = (sm[k] for k in
                                         ("dnasm", "bar", "out", "wgt", "agwr"))
    P_GLD = sm["gld"]
    P_GLD2 = sm["gld2"]

    r = Rec()
    r.auto = {"vector": V, "scalar": S}

    def mm(outap, lhsT, rhs, start, stop, sig=None, sgc=False):
        r.op("tensor", lambda e, o=outap, l=lhsT, rr=rhs, a=start, b=stop,
             g=sgc: e.matmul(out=o, lhsT=l, rhs=rr, start=a, stop=b,
                             skip_group_check=g), sig=sig)

    def tr(outap, inap, ident, sig=None):
        r.op("tensor", lambda e, o=outap, i=inap, d=ident:
             e.transpose(out=o, in_=i, identity=d), sig=sig)

    bankfree = [[V, 0] for _ in range(8)]

    def bwait(b):
        sem, tok = bankfree[b]
        r.wait("tensor", sem, tok)

    def brel(b, sem):
        bankfree[b] = [sem, sem.n]

    def rsqrt(dst, src, eps, n, scale=None):
        y = b_jnkV[:, 0:n]
        if scale is None:
            r.op("vector", lambda e, y=y, src=src: e.tensor_scalar(
                out=y, in0=src, scalar1=float(eps), scalar2=None, op0=OP.add))
        else:
            r.op("vector", lambda e, y=y, src=src: e.tensor_scalar(
                out=y, in0=src, scalar1=float(scale), scalar2=float(eps),
                op0=OP.mult, op1=OP.add))
        r.op("vector", lambda e, y=y, n=n: e.tensor_scalar(
            out=b_i32[:, 0:n], in0=y.bitcast(I32), scalar1=1, scalar2=None,
            op0=OP.arith_shift_right))
        r.op("vector", lambda e, dst=dst, n=n: e.tensor_tensor(
            out=dst.bitcast(I32), in0=c_magic[:, 0:n], in1=b_i32[:, 0:n],
            op=OP.subtract))
        t_ = b_jnkV[:, 8:8 + n]
        for _ in range(3):
            r.op("vector", lambda e, t_=t_, dst=dst: e.tensor_tensor(
                out=t_, in0=dst, in1=dst, op=OP.mult))
            r.op("vector", lambda e, t_=t_, y=y: e.tensor_tensor(
                out=t_, in0=y, in1=t_, op=OP.mult))
            r.op("vector", lambda e, t_=t_: e.tensor_scalar(
                out=t_, in0=t_, scalar1=-0.5, scalar2=1.5,
                op0=OP.mult, op1=OP.add))
            r.op("vector", lambda e, t_=t_, dst=dst: e.tensor_tensor(
                out=dst, in0=dst, in1=t_, op=OP.mult))

    def ph0():
        for key, tgt in (("m_negL", c_mL), ("m_negU", c_mU), ("m_I", c_mI),
                         ("m_cmLneg", c_cmLn), ("m_cmUpos", c_cmUp),
                         ("m_cmUneg", c_cmUn), ("m_MT", c_MT),
                         ("identB", c_idB), ("identF", c_idF),
                         ("magic", c_magic)):
            r.op("sync", lambda e, a=d_cst[key], b=tgt: e.dma_start(
                out=b[:], in_=a), sig=P_CST, sign=16)
        r.op("sync", lambda e: e.dma_start(out=c_pat, in_=d_cst["pat"]), sig=P_CST, sign=16)
        r.op("sync", lambda e: e.dma_start(out=c_convw[:], in_=d_convw), sig=P_CST, sign=16)
        r.op("sync", lambda e: e.dma_start(out=c_normwt[:], in_=d_normwt), sig=P_CST, sign=16)
        cst_tok = P_CST.n
        for dd, tt_, w in ((d_wqt, w_qT, D), (d_wkt, w_kT, D), (d_wvt, w_vT, D),
                           (d_wot, w_oT[:, :], D), (d_wbmt, w_bmT[:], 8)):
            r.op("gpsimd", lambda e, a=dd, b=tt_, w=w: e.dma_start(
                out=b.rearrange("p (it o) -> p it o", o=w),
                in_=a.rearrange("(it p) o -> p it o", p=128)), sig=P_WGT, sign=16)
        gw_tok = P_WGT.n

        r.wait("tensor", P_CST, cst_tok)
        for name, ag_off in (("up", 0), ("gate", D * OSH), ("down", 2 * D * OSH)):
            if name == "up":
                r.op("sync", lambda e: e.dma_start(
                    out=b_shard.rearrange("p (t d) -> p t d", t=3),
                    in_=d_upsh.rearrange("(t p) d -> p t d", p=128)), sig=P_SHARD, sign=16)
            elif name == "gate":
                r.wait("sync", V)
                r.wait("sync", P_AGWR)
                r.op("sync", lambda e: e.dma_start(
                    out=b_shard.rearrange("p (t d) -> p t d", t=3),
                    in_=d_gash.rearrange("(t p) d -> p t d", p=128)), sig=P_SHARD, sign=16)
            else:
                r.wait("sync", V)
                r.wait("sync", P_AGWR)
                r.op("sync", lambda e: e.dma_start(
                    out=b_shard, in_=d_dnsh), sig=P_SHARD, sign=16)
            r.wait("vector", P_SHARD)
            r.op("vector", lambda e: e.tensor_reduce(
                out=b_sst[:, 0:96],
                in_=b_shard.rearrange("p (b i) -> p b i", i=32),
                axis=AX, op=OP.max, apply_absolute_value=True))
            r.op("vector", lambda e: e.tensor_scalar(
                out=b_sst[:, 0:96], in0=b_sst[:, 0:96],
                scalar1=1.0 / 7.0, scalar2=1e-12, op0=OP.mult, op1=OP.max))
            r.op("vector", lambda e: e.reciprocal(
                out=b_sin[:, 0:96], in_=b_sst[:, 0:96]), sig=V)
            r.wait("tensor", V)
            for w_i, buf in ((0, b_sin), (1, b_sst)):
                for g in range(3):
                    tr(ps[4][0:32, 0:128], buf[:, 32 * g:32 * (g + 1)], c_idF[:],
                       sig=T)
                    r.wait("vector", T)
                    r.op("vector", lambda e, w_i=w_i, g=g: e.tensor_copy(
                        out=b_sexp[0:32, (w_i * 3 + g) * 128:(w_i * 3 + g + 1) * 128],
                        in_=ps[4][0:32, 0:128]), sig=V)
                    r.wait("tensor", V)
            for it in range(3):
                for half in range(2):
                    mm(ps[5][:, 0:512], b_sexp[0:32, it * 128:(it + 1) * 128],
                       c_pat[0:32, 512 * half:512 * (half + 1)], True, True)
                    mm(ps[6][:, 0:512], b_sexp[0:32, (3 + it) * 128:(4 + it) * 128],
                       c_pat[0:32, 512 * half:512 * (half + 1)], True, True, sig=T)
                    r.wait("vector", T)
                    sl = slice(D * it + 512 * half, D * it + 512 * (half + 1))
                    r.op("vector", lambda e, sl=sl: e.tensor_tensor(
                        out=b_shq[:, sl], in0=b_shard[:, sl], in1=ps[5][:, 0:512],
                        op=OP.mult))
                    r.op("vector", lambda e, sl=sl: e.tensor_tensor(
                        out=b_shb[:, sl], in0=b_shq[:, sl], in1=ps[6][:, 0:512],
                        op=OP.mult), sig=V)
                    r.wait("tensor", V)
            for it in range(3):
                r.wait("vector", P_AGWR)
                for ch in range(8):
                    tr(pb(7)[0:128, 0:128],
                       b_shb[:, D * it + 128 * ch:D * it + 128 * (ch + 1)],
                       c_idB[:], sig=T)
                    r.wait("vector", T)
                    r.op("vector", lambda e, ch=ch: e.tensor_copy(
                        out=b_stg[:, 128 * ch:128 * (ch + 1)],
                        in_=pb(7)[0:128, 0:128]), sig=V)
                    r.wait("tensor", V)
                r.wait("gpsimd", V)
                if name != "down":
                    r.op("gpsimd", lambda e, it=it, off=ag_off: e.dma_start(
                        out=d_agi[off:off + D * OSH]
                        .rearrange("(i o) -> i o", o=OSH)[:, 128 * it:128 * (it + 1)]
                        .rearrange("(c p) o -> p c o", p=128),
                        in_=b_stg.rearrange("p (c q) -> p c q", q=128)),
                        sig=P_AGWR, sign=16)
                else:
                    r.op("gpsimd", lambda e, it=it, off=ag_off: e.dma_start(
                        out=d_agi[off + it * D * 128:off + (it + 1) * D * 128]
                        .rearrange("(i o) -> i o", o=128)
                        .rearrange("(c p) o -> p c o", p=128),
                        in_=b_stg.rearrange("p (c q) -> p c q", q=128)),
                        sig=P_AGWR, sign=16)
        r.wait("gpsimd", P_AGWR)
        r.op("gpsimd", lambda e: e.collective_compute(
            "AllGather", OP.bypass, replica_groups=[list(range(8))],
            ins=[d_agi.tensor.ap().opt()], outs=[d_ago.tensor.ap().opt()]),
            sig=CC)

        r.wait("vector", P_CST, cst_tok)
        for col in range(96):
            r.op("vector", lambda e, c=col: e.tensor_scalar(
                out=c_diag[:, 128 * c:128 * (c + 1)], in0=c_idB[:],
                scalar1=c_convw[:, c:c + 1], scalar2=None, op0=OP.mult),
                sig=V if col == 95 else None)
        r.wait("vector", P_WGT, gw_tok)
        wov = w_oT[:].rearrange("p (it o) -> p it o", o=D)
        for it in range(8):
            r.op("vector", lambda e, it=it: e.tensor_scalar(
                out=wov[:, it, :], in0=wov[:, it, :],
                scalar1=c_normwt[:, it:it + 1], scalar2=None, op0=OP.mult),
                sig=V if it == 7 else None)

    A_state = {}
    xln_buf = [b_xln, b_o2.bitcast(BF16)[:, 0:1024]]

    def A_load(seg, tt, xgate=None):
        A_state.setdefault(seg, {"ap": [0, 0], "xt": [0, 0], "st": [0, 0]})
        t0 = seg * SEG
        px = P_X00 if tt == 0 else P_X01
        if xgate is not None:
            r.wait("sync", V, xgate)
        else:
            r.wait("sync", V)
        r.op("sync", lambda e, tt=tt, t0=t0: e.dma_start(
            out=b_x0t[:, 1024 * tt:1024 * (tt + 1)],
            in_=d_x0[t0 + 128 * tt:t0 + 128 * (tt + 1), :]), sig=px, sign=16)

    def A_stats(seg, tt):
        st = A_state[seg]
        px = P_X00 if tt == 0 else P_X01
        r.wait("scalar", px)
        r.op("scalar", lambda e, tt=tt: e.activation(
            out=b_jnkS, in_=b_x0t[:, 1024 * tt:1024 * (tt + 1)], func=AF.Copy,
            accum_out=b_acc[:, 4 * tt:4 * tt + 1]))
        r.op("scalar", lambda e, tt=tt: e.activation(
            out=b_jnkS, in_=b_x0t[:, 1024 * tt:1024 * (tt + 1)], func=AF.Square,
            accum_out=b_acc[:, 4 * tt + 1:4 * tt + 2]), sig=S)
        st["st"][tt] = S.n

    def A_load_stats(seg, tt, xgate=None):
        A_load(seg, tt, xgate)
        A_stats(seg, tt)

    def A_apply(seg, tt):
        st = A_state[seg]
        r.wait("vector", S, st["st"][tt])
        m = b_acc[:, 8 + 4 * tt:9 + 4 * tt]
        va = b_acc[:, 9 + 4 * tt:10 + 4 * tt]
        r.op("vector", lambda e, tt=tt, m=m: e.tensor_scalar(
            out=m, in0=b_acc[:, 4 * tt:4 * tt + 1], scalar1=1.0 / D,
            scalar2=None, op0=OP.mult))
        r.op("vector", lambda e, m=m: e.tensor_tensor(
            out=b_jnkV[:, 16:17], in0=m, in1=m, op=OP.mult))
        r.op("vector", lambda e, tt=tt, va=va: e.scalar_tensor_tensor(
            out=va, in0=b_acc[:, 4 * tt + 1:4 * tt + 2], scalar=1.0 / D,
            in1=b_jnkV[:, 16:17], op0=OP.mult, op1=OP.subtract))
        rr = b_acc[:, 10 + 4 * tt:11 + 4 * tt]
        rsqrt(rr, va, 1e-5, 1)
        r.op("vector", lambda e, tt=tt, m=m, rr=rr: e.tensor_scalar(
            out=xln_buf[tt], in0=b_x0t[:, 1024 * tt:1024 * (tt + 1)],
            scalar1=m, scalar2=rr, op0=OP.subtract, op1=OP.mult), sig=V)
        st["ap"][tt] = V.n

    def A_transpose(seg, tt, bank):
        st = A_state[seg]
        xtv = b_xT.rearrange("p (c x) -> p c x", x=SEG)
        bwait(bank)
        r.wait("tensor", V, st["ap"][tt])
        for ch in range(8):
            tr(pb(bank)[0:128, 128 * ch:128 * (ch + 1)],
               xln_buf[tt][:, 128 * ch:128 * (ch + 1)], c_idB[:],
               sig=T if ch == 7 else None)
        tr_tok = T.n
        r.wait("scalar", T, tr_tok)
        r.op("scalar", lambda e, tt=tt, bank=bank: e.copy(
            out=xtv[:, :, 128 * tt:128 * (tt + 1)],
            in_=pb(bank)[:, 0:1024].rearrange("p (c x) -> p c x", x=128)),
            sig=S)
        st["xt"][tt] = S.n
        brel(bank, S)

    def attn_segment(seg, has_next):
        t0 = seg * SEG
        prx = b_praw.rearrange("p (r c x) -> p r c x", r=3, c=8)
        xtv = b_xT.rearrange("p (c x) -> p c x", x=SEG)
        khfv = b_khf.rearrange("p (c x) -> p c x", x=SEG)
        kbfv = b_kbf.rearrange("p (c x) -> p c x", x=SEG)
        qfv = b_qf.rearrange("p (c x) -> p c x", x=SEG)

        if seg == 0:
            r.op("vector", lambda e: e.memset(prx[:, :, :, 0:4], 0.0), sig=V)
        else:
            r.op("vector", lambda e: e.tensor_copy(
                out=prx[:, :, :, 1:4], in_=prx[:, :, :, 257:260]), sig=V)
        prb_tok = V.n

        if seg == 0:
            for tt in range(NTT):
                A_load_stats(seg, tt)
            for tt in range(NTT):
                A_apply(seg, tt)
            A_transpose(seg, 0, 2)
            A_transpose(seg, 1, 3)
        xt_tok = A_state[seg]["xt"]

        for tt in range(NTT):
            bwait(0)
            r.wait("tensor", S, xt_tok[tt])
            for it in range(8):
                mm(ps[0][:, 0:8], xtv[:, it, 128 * tt:128 * (tt + 1)],
                   w_bmT[:].rearrange("p (c x) -> p c x", x=8)[:, it, :],
                   it == 0, it == 7, sig=T if it == 7 else None)
            r.wait("scalar", T)
            r.op("scalar", lambda e, tt=tt: e.activation(
                out=b_bm[:, 8 * tt:8 * (tt + 1)], in_=ps[0][:, 0:8],
                func=AF.Sigmoid), sig=S)
            brel(0, S)

        r.wait("tensor", V, prb_tok)
        pr_tok = [[0] * 8 for _ in range(3)]
        for pi, wT in enumerate((w_qT, w_kT, w_vT)):
            wv = wT.rearrange("p (i o) -> p i o", o=D)
            for c in range(8):
                b = c % 2
                bwait(b)
                for it in range(8):
                    mm(ps[b][:, 0:256], wv[:, it, 128 * c:128 * (c + 1)],
                       xtv[:, it, :], it == 0, it == 7,
                       sig=T if it == 7 else None)
                mm_tok = T.n
                r.wait("vector", T, mm_tok)
                r.op("vector", lambda e, pi=pi, c=c, b=b: e.tensor_copy(
                    out=prx[:, pi, c, 4:260], in_=ps[b][:, 0:256]), sig=V)
                pr_tok[pi][c] = V.n
                brel(b, V)
        for pi, dst in enumerate((b_qf, b_ksil, b_vsil)):
            dv = dst.rearrange("p (c x) -> p c x", x=SEG)
            for c in range(8):
                b = 2 + (c % 2)
                bwait(b)
                r.wait("tensor", V, pr_tok[pi][c])
                for tap in range(4):
                    col = (pi * 8 + c) * 4 + tap
                    mm(ps[b][:, 0:256], c_diag[:, 128 * col:128 * (col + 1)],
                       prx[:, pi, c, 1 + tap:257 + tap],
                       tap == 0, tap == 3, sig=T if tap == 3 else None)
                mm_tok = T.n
                r.wait("scalar", T, mm_tok)
                r.op("scalar", lambda e, dv=dv, c=c, b=b: e.activation(
                    out=dv[:, c, :], in_=ps[b][:, 0:256], func=AF.Silu), sig=S)
                brel(b, S)

        r.wait("tensor", S)
        ksv = b_ksil.rearrange("p (c x) -> p c x", x=SEG)
        vsv = b_vsil.rearrange("p (c x) -> p c x", x=SEG)
        e_bq = (2, 0)
        e_bk = (3, 1)
        sq_tok = [0, 0]
        qsq_tok = [0, 0]
        kv_tok = [0, 0]

        def E_qk(tt):
            bq, bk = e_bq[tt], e_bk[tt]
            ac = 16 + 8 * tt
            bwait(bq)
            for ch in range(8):
                tr(pb(bq)[0:128, 128 * ch:128 * (ch + 1)],
                   qfv[:, ch, 128 * tt:128 * (tt + 1)], c_idB[:],
                   sig=T if ch == 7 else None)
            qtr = T.n
            bwait(bk)
            for ch in range(8):
                tr(pb(bk)[0:128, 128 * ch:128 * (ch + 1)],
                   ksv[:, ch, 128 * tt:128 * (tt + 1)], c_idB[:],
                   sig=T if ch == 7 else None)
            ktr = T.n
            r.wait("scalar", T, qtr)
            for h in range(H):
                r.op("scalar", lambda e, h=h, bq=bq, ac=ac: e.activation(
                    out=b_jnkS[:, 0:256], in_=pb(bq)[:, 256 * h:256 * (h + 1)],
                    func=AF.Square, accum_out=b_acc[:, ac + h:ac + h + 1]),
                    sig=S if h == 3 else None)
            qsq_tok[tt] = S.n
            brel(bq, S)
            r.wait("scalar", T, ktr)
            for h in range(H):
                r.op("scalar", lambda e, h=h, bk=bk, ac=ac: e.activation(
                    out=b_jnkS[:, 0:256], in_=pb(bk)[:, 256 * h:256 * (h + 1)],
                    func=AF.Square, accum_out=b_acc[:, ac + 4 + h:ac + 5 + h]),
                    sig=S if h == 3 else None)
            sq_tok[tt] = S.n

        def E_chain(tt):
            bq, bk = e_bq[tt], e_bk[tt]
            ac = 16 + 8 * tt
            r.wait("vector", S, sq_tok[tt])
            rsqrt(b_rn[:, 16 * tt:16 * tt + 4], b_acc[:, ac:ac + 4], 1e-6, 4)
            rsqrt(b_rn[:, 16 * tt + 4:16 * tt + 8], b_acc[:, ac + 4:ac + 8],
                  1e-6, 4)
            r.op("vector", lambda e, tt=tt: e.tensor_tensor(
                out=b_bx[:, 8 * tt:8 * tt + 4],
                in0=b_rn[:, 16 * tt + 4:16 * tt + 8],
                in1=b_bm[:, 8 * tt:8 * tt + 4], op=OP.mult))
            r.op("vector", lambda e, tt=tt: e.tensor_scalar(
                out=b_bx[:, 8 * tt + 4:8 * tt + 8],
                in0=b_bm[:, 8 * tt + 4:8 * tt + 8],
                scalar1=-1.0, scalar2=1.0, op0=OP.mult, op1=OP.add))
            r.op("vector", lambda e, tt=tt: e.tensor_tensor(
                out=b_rn[:, 16 * tt + 8:16 * tt + 12],
                in0=b_rn[:, 16 * tt:16 * tt + 4],
                in1=b_bm[:, 8 * tt + 4:8 * tt + 8], op=OP.mult))
            for h in range(H):
                r.op("vector", lambda e, tt=tt, h=h, bk=bk: e.tensor_scalar(
                    out=b_ktok[:, 1024 * tt + 256 * h:1024 * tt + 256 * (h + 1)],
                    in0=pb(bk)[:, 256 * h:256 * (h + 1)],
                    scalar1=b_rn[:, 16 * tt + 4 + h:16 * tt + 5 + h],
                    scalar2=None, op0=OP.mult))
                r.op("vector", lambda e, tt=tt, h=h, bk=bk: e.tensor_scalar(
                    out=b_kbtok[:, 1024 * tt + 256 * h:1024 * tt + 256 * (h + 1)],
                    in0=pb(bk)[:, 256 * h:256 * (h + 1)],
                    scalar1=b_bx[:, 8 * tt + h:8 * tt + 1 + h],
                    scalar2=None, op0=OP.mult), sig=V if h == 3 else None)
            kv_tok[tt] = V.n
            brel(bk, V)

        def E_v_back(tt):
            bq, bk = e_bq[tt], e_bk[tt]
            bwait(bq)
            for ch in range(8):
                tr(pb(bq)[0:128, 128 * ch:128 * (ch + 1)],
                   vsv[:, ch, 128 * tt:128 * (tt + 1)], c_idB[:],
                   sig=T if ch == 7 else None)
            vtr = T.n
            r.wait("vector", T, vtr)
            r.op("vector", lambda e, tt=tt, bq=bq: e.tensor_copy(
                out=b_vtok[:, 1024 * tt:1024 * (tt + 1)], in_=pb(bq)[:, 0:1024]))
            for h in range(H):
                r.op("vector", lambda e, tt=tt, h=h, bq=bq: e.tensor_scalar(
                    out=b_vbtok[:, 1024 * tt + 256 * h:1024 * tt + 256 * (h + 1)],
                    in0=pb(bq)[:, 256 * h:256 * (h + 1)],
                    scalar1=b_bm[:, 8 * tt + h:8 * tt + 1 + h],
                    scalar2=None, op0=OP.mult), sig=V if h == 3 else None)
            brel(bq, V)
            bwait(bk)
            for src, dstf in ((b_ktok, b_khf), (b_kbtok, b_kbf)):
                for ch in range(8):
                    tr(pb(bk)[0:128, 128 * ch:128 * (ch + 1)],
                       src[:, 1024 * tt + 128 * ch:1024 * tt + 128 * (ch + 1)],
                       c_idB[:], sig=T if ch == 7 else None)
                r.wait("vector", T)
                r.op("vector", lambda e, tt=tt, dstf=dstf, bk=bk: e.tensor_copy(
                    out=dstf.rearrange("p (c x) -> p c x", x=SEG)
                    [:, :, 128 * tt:128 * (tt + 1)],
                    in_=pb(bk)[:, 0:1024].rearrange("p (c x) -> p c x", x=128)),
                    sig=V)
                r.wait("tensor", V)
            brel(bk, V)

        E_qk(0)
        E_qk(1)
        E_chain(0)
        E_v_back(0)
        E_chain(1)
        E_v_back(1)

        r.wait("tensor", V)
        lad_tok = [0] * H
        for h in range(H):
            bank = ps[h % 2]
            for blk in range(NTT):
                ts = slice(128 * blk, 128 * (blk + 1))
                bwait(h % 2)
                for ci, cc in enumerate((2 * h, 2 * h + 1)):
                    mm(bank[:, 0:128], kbfv[:, cc, ts], khfv[:, cc, ts],
                       ci == 0, ci == 1)
                for ci, cc in enumerate((2 * h, 2 * h + 1)):
                    mm(bank[:, 128:256], khfv[:, cc, ts], kbfv[:, cc, ts],
                       ci == 0, ci == 1, sig=T if ci == 1 else None)
                gr_tok = T.n
                r.wait("vector", T, gr_tok)
                bs = slice(128 * blk, 128 * (blk + 1))
                r.op("vector", lambda e, h=h, bs=bs, bank=bank: e.tensor_tensor(
                    out=b_Pp[h][:, bs], in0=bank[:, 0:128], in1=c_mL[:],
                    op=OP.mult))
                r.op("vector", lambda e, h=h, bs=bs, bank=bank: e.tensor_tensor(
                    out=b_Mp[h][:, bs], in0=bank[:, 128:256], in1=c_mU[:],
                    op=OP.mult))
                r.op("vector", lambda e, h=h, bs=bs: e.tensor_tensor(
                    out=b_Mp[h][:, bs], in0=b_Mp[h][:, bs], in1=c_mI[:],
                    op=OP.add), sig=V)
                brel(h % 2, V)
            for lev in range(4):
                r.op("vector", lambda e, h=h: e.transpose(
                    out=b_Qp[h][:, :], in_=b_Pp[h][:, :]), sig=V)
                vt_tok = V.n
                r.wait("tensor", V, vt_tok)
                bwait(h % 2)
                for blk in range(NTT):
                    bs = slice(128 * blk, 128 * (blk + 1))
                    mm(bank[:, 128 * blk:128 * (blk + 1)],
                       b_Qp[h][:, bs], b_Pp[h][:, bs], True, True,
                       sig=T if blk == NTT - 1 else None)
                sq_tok = T.n
                r.wait("scalar", T, sq_tok)
                r.op("scalar", lambda e, h=h, bank=bank: e.copy(
                    out=b_Pp[h][:, :], in_=bank[:, 0:256]), sig=S)
                pe_tok = S.n
                r.wait("tensor", S, pe_tok)
                for blk in range(NTT):
                    bs = slice(128 * blk, 128 * (blk + 1))
                    mm(bank[:, 256 + 128 * blk:384 + 128 * blk],
                       b_Pp[h][:, bs], b_Mp[h][:, bs], True, True,
                       sig=T if blk == NTT - 1 else None)
                pr_tok2 = T.n
                r.wait("vector", T, pr_tok2)
                r.op("vector", lambda e, h=h, bank=bank: e.tensor_tensor(
                    out=b_Mp[h][:, :], in0=bank[:, 256:512], in1=b_Mp[h][:, :],
                    op=OP.add), sig=V)
                brel(h % 2, V)
            lad_tok[h] = V.n

        sup_tok = [0] * H
        onrm_tok = [0] * NTT
        for blk in range(NTT):
            ts = slice(128 * blk, 128 * (blk + 1))
            first = (seg == 0 and blk == 0)
            for h in range(H):
                p = h % 2
                bankA = ps[p]
                bankU = ps[2 + p]
                r.wait("tensor", V, lad_tok[h])
                if not first:
                    r.wait("scalar", T, sup_tok[h] if sup_tok[h] else T.n)
                    r.op("scalar", lambda e, h=h: e.copy(
                        out=b_S[h][:, :], in_=ps[4 + h][:, 0:512]), sig=S)
                    sc_tok = S.n
                    r.wait("tensor", S, sc_tok)
                bwait(2 + p)
                mm(bankU[:, 0:256], b_Mp[h][:, ts],
                   b_vbtok[:, 1024 * blk + 256 * h:1024 * blk + 256 * (h + 1)],
                   True, first)
                bwait(p)
                for hf in range(2):
                    mm(bankA[:, 128 * hf:128 * (hf + 1)],
                       b_kbtok[:, 1024 * blk + 256 * h + 128 * hf:
                               1024 * blk + 256 * h + 128 * (hf + 1)],
                       b_Mp[h][:, ts], hf == 0, hf == 1,
                       sig=T if hf == 1 else None)
                wt_tok = T.n
                r.wait("vector", T, wt_tok)
                r.op("vector", lambda e, h=h, bankA=bankA: e.tensor_scalar(
                    out=b_nw[h][:, :], in0=bankA[:, 0:256],
                    scalar1=-1.0, scalar2=None, op0=OP.mult), sig=V)
                nw_tok = V.n
                r.wait("tensor", V, nw_tok)
                nwv = b_nw[h]
                if not first:
                    for hf in range(2):
                        mm(bankU[:, 0:256], nwv[:, 128 * hf:128 * (hf + 1)],
                           b_S[h][:, 256 * hf:256 * (hf + 1)], False, hf == 1)
                for hf in range(2):
                    mm(bankA[:, 256:384], nwv[:, 128 * hf:128 * (hf + 1)],
                       khfv[:, 2 * h + hf, ts], hf == 0, hf == 1)
                for hf in range(2):
                    mm(bankA[:, 384:512], khfv[:, 2 * h + hf, ts],
                       nwv[:, 128 * hf:128 * (hf + 1)], hf == 0, hf == 1,
                       sig=T if hf == 1 else None)
                lr_tok = T.n
                r.wait("vector", T, lr_tok)
                r.op("vector", lambda e, h=h, bankU=bankU: e.tensor_copy(
                    out=b_U[h][:, :], in_=bankU[:, 0:256]))
                r.op("vector", lambda e, h=h, bankA=bankA: e.tensor_tensor(
                    out=b_Lt[h][:, :], in0=bankA[:, 256:384], in1=c_cmLn[:],
                    op=OP.mult))
                r.op("vector", lambda e, h=h, bankA=bankA: e.tensor_tensor(
                    out=b_LtTp[h][:, :], in0=bankA[:, 384:512], in1=c_cmUn[:],
                    op=OP.mult))
                r.op("vector", lambda e, h=h, bankA=bankA: e.tensor_tensor(
                    out=b_LtTn[h][:, :], in0=bankA[:, 384:512], in1=c_cmUp[:],
                    op=OP.mult), sig=V)
                msk_tok = V.n
                r.wait("tensor", V, msk_tok)
                mm(bankA[:, 0:128], b_Lt[h][:, :], b_LtTp[h][:, :], True, True,
                   sig=T)
                l2_tok = T.n
                r.wait("vector", T, l2_tok)
                r.op("vector", lambda e, h=h, bankA=bankA: e.tensor_copy(
                    out=b_L2T[h][:, :], in_=bankA[:, 0:128]), sig=V)
                l2e_tok = V.n
                r.wait("tensor", V, l2e_tok)
                mm(bankU[:, 256:512], b_L2T[h][:, :], b_U[h][:, :], True, True,
                   sig=T)
                t1_tok = T.n
                r.wait("vector", T, t1_tok)
                r.op("vector", lambda e, h=h, bankU=bankU: e.tensor_tensor(
                    out=b_U[h][:, :], in0=bankU[:, 256:512], in1=b_U[h][:, :],
                    op=OP.add), sig=V)
                t1e_tok = V.n
                r.wait("tensor", V, t1e_tok)
                mm(bankU[:, 0:256], b_LtTn[h][:, :], b_U[h][:, :], True, True,
                   sig=T)
                up_tok2 = T.n
                r.wait("vector", T, up_tok2)
                r.op("vector", lambda e, h=h, bankU=bankU: e.tensor_tensor(
                    out=b_U[h][:, :], in0=bankU[:, 0:256], in1=b_U[h][:, :],
                    op=OP.add), sig=V)
                upe_tok = V.n
                brel(2 + p, V)
                r.wait("tensor", V, upe_tok)
                for hf in range(2):
                    mm(bankA[:, 128:256], khfv[:, 2 * h + hf, ts],
                       qfv[:, 2 * h + hf, ts], hf == 0, hf == 1,
                       sig=T if hf == 1 else None)
                gt_tok = T.n
                r.wait("vector", T, gt_tok)
                r.op("vector", lambda e, h=h, bankA=bankA: e.tensor_tensor(
                    out=b_GT[h][:, :], in0=bankA[:, 128:256], in1=c_MT[:],
                    op=OP.mult), sig=V)
                gte_tok = V.n
                r.wait("tensor", V, gte_tok)
                if not first:
                    for hf in range(2):
                        mm(bankA[:, 256:512], qfv[:, 2 * h + hf, ts],
                           b_S[h][:, 256 * hf:256 * (hf + 1)], hf == 0, False)
                    mm(bankA[:, 256:512], b_GT[h][:, :], b_U[h][:, :], False,
                       True, sig=T)
                else:
                    mm(bankA[:, 256:512], b_GT[h][:, :], b_U[h][:, :], True,
                       True, sig=T)
                o_tok = T.n
                for hf in range(2):
                    mm(ps[4 + h][:, 256 * hf:256 * (hf + 1)],
                       b_ktok[:, 1024 * blk + 256 * h + 128 * hf:
                               1024 * blk + 256 * h + 128 * (hf + 1)],
                       b_U[h][:, :], first and hf == 0, False,
                       sig=T if hf == 1 else None, sgc=True)
                sup_tok[h] = T.n
                r.wait("vector", T, o_tok)
                r.op("vector", lambda e, blk=blk, h=h: e.tensor_scalar(
                    out=b_jnkS[:, 0:256],
                    in0=b_vtok[:, 1024 * blk + 256 * h:1024 * blk + 256 * (h + 1)],
                    scalar1=b_bx[:, 8 * blk + 4 + h:8 * blk + 5 + h],
                    scalar2=None, op0=OP.mult))
                r.op("vector", lambda e, blk=blk, h=h, bankA=bankA:
                     e.scalar_tensor_tensor(
                         out=b_o2[:, 256 * h:256 * (h + 1)],
                         in0=bankA[:, 256:512],
                         scalar=b_rn[:, 16 * blk + 8 + h:16 * blk + 9 + h],
                         in1=b_jnkS[:, 0:256], op0=OP.mult, op1=OP.add), sig=V)
                o2_tok = V.n
                brel(p, V)
                r.wait("scalar", V, o2_tok)
                r.op("scalar", lambda e, h=h: e.activation(
                    out=b_jnkS[:, 256:512], in_=b_o2[:, 256 * h:256 * (h + 1)],
                    func=AF.Square, accum_out=b_acc[:, 32 + h:33 + h]),
                    sig=S if h == 3 else None)
            r.wait("vector", S)
            rsqrt(b_rn[:, 16 * blk + 12:16 * blk + 16], b_acc[:, 32:36],
                  1e-5, 4, scale=1.0 / 256.0)
            for h in range(H):
                r.op("vector", lambda e, blk=blk, h=h: e.tensor_scalar(
                    out=b_onrm[:, 1024 * blk + 256 * h:1024 * blk + 256 * (h + 1)],
                    in0=b_o2[:, 256 * h:256 * (h + 1)],
                    scalar1=b_rn[:, 16 * blk + 12 + h:16 * blk + 13 + h],
                    scalar2=None, op0=OP.mult), sig=V if h == 3 else None)
            onrm_tok[blk] = V.n

        for tt in range(NTT):
            gidx = seg * NTT + tt
            r.wait("tensor", V, onrm_tok[tt])
            bwait(2)
            for ch in range(8):
                tr(pb(2)[0:128, 128 * ch:128 * (ch + 1)],
                   b_onrm[:, 1024 * tt + 128 * ch:1024 * tt + 128 * (ch + 1)],
                   c_idB[:], sig=T if ch == 7 else None)
            r.wait("scalar", T)
            r.op("scalar", lambda e: e.copy(
                out=b_oT, in_=pb(2)[:, 0:1024]), sig=S)
            brel(2, S)
            r.wait("tensor", S)
            bwait(0)
            bwait(1)
            for nh in range(2):
                for ch in range(8):
                    mm(ps[nh][:, 0:512], b_oT[:, 128 * ch:128 * (ch + 1)],
                       w_oT[:].rearrange("p (i o) -> p i o", o=D)
                       [:, ch, 512 * nh:512 * (nh + 1)],
                       ch == 0, ch == 7, sig=T if (nh == 1 and ch == 7) else None)
            r.wait("vector", T)
            for nh in range(2):
                r.op("vector", lambda e, tt=tt, nh=nh: e.tensor_tensor(
                    out=b_x1[:, 512 * nh:512 * (nh + 1)], in0=ps[nh][:, 0:512],
                    in1=b_x0t[:, 1024 * tt + 512 * nh:1024 * tt + 512 * (nh + 1)],
                    op=OP.add), sig=V if nh == 1 else None)
            x1ev = V.n
            brel(0, V)
            brel(1, V)
            if has_next:
                A_load(seg + 1, tt, xgate=x1ev)
            r.wait("scalar", V)
            r.op("scalar", lambda e: e.activation(
                out=b_jnkS, in_=b_x1[:], func=AF.Copy,
                accum_out=b_acc[:, 40:41]))
            r.op("scalar", lambda e: e.activation(
                out=b_jnkS, in_=b_x1[:], func=AF.Square,
                accum_out=b_acc[:, 41:42]), sig=S)
            ln2_tok = S.n
            if has_next:
                A_stats(seg + 1, tt)
            r.wait("vector", S, ln2_tok)
            m2 = b_ln2[:, (2 * gidx) % 32:(2 * gidx) % 32 + 1]
            r2 = b_ln2[:, (2 * gidx) % 32 + 1:(2 * gidx) % 32 + 2]
            r.op("vector", lambda e, m2=m2: e.tensor_scalar(
                out=m2, in0=b_acc[:, 40:41], scalar1=1.0 / D, scalar2=None,
                op0=OP.mult))
            r.op("vector", lambda e, m2=m2: e.tensor_tensor(
                out=b_jnkV[:, 16:17], in0=m2, in1=m2, op=OP.mult))
            r.op("vector", lambda e: e.scalar_tensor_tensor(
                out=b_acc[:, 42:43], in0=b_acc[:, 41:42], scalar=1.0 / D,
                in1=b_jnkV[:, 16:17], op0=OP.mult, op1=OP.subtract))
            rsqrt(r2, b_acc[:, 42:43], 1e-5, 1)
            r.op("vector", lambda e, m2=m2, r2=r2: e.tensor_scalar(
                out=b_xr[:], in0=b_x1[:], scalar1=m2, scalar2=r2,
                op0=OP.subtract, op1=OP.mult))
            r.op("vector", lambda e, gidx=gidx: e.tensor_reduce(
                out=b_mxp[:, gidx % 16:gidx % 16 + 1], in_=b_xr[:], axis=AX,
                op=OP.max, apply_absolute_value=True), sig=V)
            if gidx == 0:
                r.op("vector", lambda e: e.tensor_copy(
                    out=b_mxp[:, 16:17], in_=b_mxp[:, 0:1]), sig=V)
            else:
                r.op("vector", lambda e, gidx=gidx: e.tensor_tensor(
                    out=b_mxp[:, 16:17], in0=b_mxp[:, 16:17],
                    in1=b_mxp[:, gidx % 16:gidx % 16 + 1], op=OP.max), sig=V)
            r.wait("sync", V)
            r.op("sync", lambda e, tt=tt: e.dma_start(
                out=d_x1sp[t0 + 128 * tt:t0 + 128 * (tt + 1), :], in_=b_x1[:]),
                sig=P_X1ST, sign=16)
            r.wait("vector", P_X1ST)
            if has_next:
                A_apply(seg + 1, tt)
        if has_next:
            A_transpose(seg + 1, 0, 3)
            A_transpose(seg + 1, 1, 2)

    def fence():
        for eng in _ENGS:
            for sem in (T, V, S, P_CST, P_SHARD, P_X00, P_X01, P_X1ST,
                        P_X1LD, P_UPSP, P_UPLD, P_WST[0], P_WST[1], P_DN,
                        P_BAR, P_OUT, P_WGT, P_AGWR, P_GLD, P_GLD2):
                r.wait(eng, sem)

    def barrier_max2(slot, col_ap, n=1):
        mi, mo = (d_mxi, d_mxo) if n == 1 else (d_mxi2, d_mxo2)
        r.wait("sync", V)
        r.wait("sync", P_BAR)
        if n == 1:
            r.op("sync", lambda e: e.dma_start(
                out=d_mxi, in_=col_ap), sig=P_BAR, sign=16)
        else:
            for s in range(n):
                r.op("sync", lambda e, s=s: e.dma_start(
                    out=d_mxi2[s:s + 1, :], in_=col_ap[:, s:s + 1]),
                    sig=P_BAR, sign=16)
        r.wait("gpsimd", P_BAR)
        r.op("gpsimd", lambda e, mi=mi, mo=mo: e.collective_compute(
            "AllReduce", OP.max, replica_groups=[list(range(8))],
            ins=[mi.tensor.ap().opt()], outs=[mo.tensor.ap().opt()]),
            sig=CC)
        r.wait("sync", CC)
        if n == 1:
            r.op("sync", lambda e: e.dma_start(
                out=b_row[0:1, 0:128],
                in_=d_mxo.rearrange("p one -> one p")), sig=P_BAR, sign=16)
        else:
            r.op("sync", lambda e: e.dma_start(
                out=b_row[0:2, 0:128], in_=d_mxo2), sig=P_BAR, sign=16)
        r.wait("vector", P_BAR)
        r.op("vector", lambda e, n=n: e.tensor_reduce(
            out=b_sc1[0:n, 0:1], in_=b_row[0:n, 0:128], axis=AX, op=OP.max),
            sig=V)
        r.wait("sync", V)
        r.op("sync", lambda e, slot=slot, n=n: e.dma_start(
            out=d_sc[0:1, slot:slot + n], in_=b_sc1[0:n, 0:1]), sig=P_BAR, sign=16)
        r.wait("sync", P_BAR)
        r.op("sync", lambda e, slot=slot, n=n: e.dma_start(
            out=b_rs[:, slot:slot + n],
            in_=d_sc[0:1, slot:slot + n].broadcast_to((128, n))), sig=P_BAR, sign=16)
        r.wait("vector", P_BAR)
        r.wait("scalar", P_BAR)

    def ph2():
        fence()
        r.wait("sync", CC)
        r.wait("scalar", CC)
        for oc in range(2):
            sc_, lo = oc // 3, (oc % 3) * 128
            for wi, woff in ((0, 0), (1, D * OSH)):
                r.op("scalar", lambda e, sc_=sc_, lo=lo, woff=woff, wi=wi,
                     oc=oc: e.dma_start(
                         out=_wst(oc % 2, wi).rearrange("p (c x) -> p c x", x=128),
                         in_=d_ago[sc_, woff:woff + D * OSH]
                         .rearrange("(i o) -> i o", o=OSH)[:, lo:lo + 128]
                         .rearrange("(c p) o -> p c o", p=128)),
                     sig=P_WST[oc % 2], sign=16, dma=True)
        x1t = [b_x1[:], b_jnkS[:]]
        xlnb = [b_xln[:], b_oT[:]]
        ld2_tok = [0] * NTILE
        cons_tok = [0] * NTILE
        tr_tok = [0] * NTILE
        cp_tok = [0] * NTILE

        def Ld2(t):
            pool = (P_X1LD, P_X1ST)[t % 2]
            if t >= 2:
                r.wait("scalar", V, cons_tok[t - 2])
            r.op("scalar", lambda e, t=t: e.dma_start(
                out=x1t[t % 2], in_=d_x1sp[128 * t:128 * (t + 1), :]),
                sig=pool, sign=16, dma=True)
            ld2_tok[t] = pool.n

        Ld2(0)
        Ld2(1)
        barrier_max2(0, b_mxp[:, 16:17])
        r.op("vector", lambda e: e.tensor_scalar(
            out=b_rs[:, 1:2], in0=b_rs[:, 0:1], scalar1=1.0 / 32767.0,
            scalar2=1e-12, op0=OP.mult, op1=OP.max))
        r.op("vector", lambda e: e.reciprocal(
            out=b_rs[:, 2:3], in_=b_rs[:, 1:2]), sig=V)
        for tile in range(NTILE):
            m2 = b_ln2[:, (2 * tile) % 32:(2 * tile) % 32 + 1]
            r2 = b_ln2[:, (2 * tile) % 32 + 1:(2 * tile) % 32 + 2]
            r.wait("vector", (P_X1LD, P_X1ST)[tile % 2], ld2_tok[tile])
            r.op("vector", lambda e, r2=r2: e.tensor_tensor(
                out=b_jnkV[:, 0:1], in0=r2, in1=b_rs[:, 2:3], op=OP.mult))
            r.op("vector", lambda e, m2=m2, tile=tile: e.tensor_scalar(
                out=b_xr[:], in0=x1t[tile % 2], scalar1=m2,
                scalar2=b_jnkV[:, 0:1], op0=OP.subtract, op1=OP.mult))
            r.op("vector", lambda e: e.tensor_scalar(
                out=b_o2.bitcast(I16)[:, 0:1024], in0=b_xr[:],
                scalar1=32767.0, scalar2=None, op0=OP.min))
            if tile >= 2:
                r.wait("vector", T, tr_tok[tile - 2])
            r.op("vector", lambda e, tile=tile: e.tensor_copy(
                out=xlnb[tile % 2], in_=b_o2.bitcast(I16)[:, 0:1024]), sig=V)
            cons_tok[tile] = V.n
            if tile + 2 < NTILE:
                Ld2(tile + 2)
            bk = 2 + tile % 2
            r.wait("tensor", V, cons_tok[tile])
            if tile >= 2:
                r.wait("tensor", S, cp_tok[tile - 2])
            for ch in range(8):
                tr(pb(bk)[0:128, 128 * ch:128 * (ch + 1)],
                   xlnb[tile % 2][:, 128 * ch:128 * (ch + 1)], c_idB[:],
                   sig=T if ch == 7 else None)
            tr_tok[tile] = T.n
            r.wait("scalar", T, tr_tok[tile])
            r.op("scalar", lambda e, tile=tile, bk=bk: e.copy(
                out=xqT_A[:, 0:10240].rearrange("p (c x) -> p c x", x=2048)
                [:, :, 128 * tile:128 * (tile + 1)],
                in_=pb(bk)[:, 0:640].rearrange("p (c x) -> p c x", x=128)))
            r.op("scalar", lambda e, tile=tile, bk=bk: e.copy(
                out=xqT_B[:, 0:6144].rearrange("p (c x) -> p c x", x=2048)
                [:, :, 128 * tile:128 * (tile + 1)],
                in_=pb(bk)[:, 640:1024].rearrange("p (c x) -> p c x", x=128)),
                sig=S)
            cp_tok[tile] = S.n

    def _wst(dbuf, wi):
        return b_wst[:, 2048 * dbuf + 1024 * wi:2048 * dbuf + 1024 * (wi + 1)]

    def ph3():
        n_oc = int(os.environ.get("BASSMK_F1OC", str(NOC)))
        r.wait("gpsimd", CC)
        r.wait("gpsimd", T)
        for oc in range(NOC):
            for c8 in range(8):
                r.op("gpsimd", lambda e, oc=oc, c8=c8: e.dma_start(
                    out=w_dnT[:, D * oc + 128 * c8:D * oc + 128 * (c8 + 1)],
                    in_=d_ago[c8, 2 * D * OSH + 128 * oc * 128:
                              2 * D * OSH + 128 * (oc + 1) * 128]
                    .rearrange("(p o) -> p o", o=128)), sig=P_DN, sign=16)
        dn_tok = P_DN.n
        r.wait("vector", S)
        r.op("vector", lambda e: e.memset(b_mxp[:, 17:19], 0.0), sig=V)
        last_mm_tok = [0, 0]
        wst_tok = [P_WST[0].n, P_WST[1].n]
        sp_tok = [0] * 4
        s0 = S.n
        gen_free = [[S, s0]] * 8
        gen = 0
        for oc in range(n_oc):
            dbuf = oc % 2
            r.wait("tensor", P_WST[dbuf], wst_tok[dbuf])
            for half in range(2):
                pbase = 4 * (gen % 2)
                for b4 in range(4):
                    sem, tok = gen_free[pbase + b4]
                    r.wait("tensor", sem, tok)
                for wi in range(2):
                    for th in range(2):
                        bank = ps[pbase + 2 * wi + th]
                        for it in range(8):
                            mm(bank[:, 0:512],
                               _wst(dbuf, wi)[:, 128 * it:128 * (it + 1)],
                               xq_sl(it, 1024 * half + 512 * th, 512),
                               it == 0, it == 7,
                               sig=T if (wi == 1 and th == 1 and it == 7)
                               else None)
                mm_tok = T.n
                last_mm_tok[dbuf] = mm_tok
                r.wait("scalar", T, mm_tok)
                spool = (P_UPSP, P_UPLD)[half]
                r.wait("scalar", spool, sp_tok[half])
                for wi in range(2):
                    for th in range(2):
                        bank = ps[pbase + 2 * wi + th]
                        if wi == 0:
                            r.op("scalar", lambda e, th=th, half=half,
                                 bank=bank: e.copy(
                                     out=v_fst[:, 1024 * half + 512 * th:
                                               1024 * half + 512 * (th + 1)],
                                     in_=bank[:, 0:512]), sig=S)
                        else:
                            r.op("scalar", lambda e, oc=oc, th=th, half=half,
                                 bank=bank: e.copy(
                                     out=g_slot(oc)
                                     [:, 1024 * half + 512 * th:
                                      1024 * half + 512 * (th + 1)],
                                     in_=bank[:, 0:512]), sig=S)
                cp_tok = S.n
                r.wait("sync", S, cp_tok)
                r.op("sync", lambda e, oc=oc, half=half: e.dma_start(
                    out=d_upsp[oc][:, 1024 * half:1024 * (half + 1)],
                    in_=v_fst[:, 1024 * half:1024 * (half + 1)]),
                    sig=spool, sign=16)
                sp_tok[half] = spool.n
                r.wait("vector", S, cp_tok)
                for wi in range(2):
                    for th in range(2):
                        bank = ps[pbase + 2 * wi + th]
                        r.op("vector", lambda e, bank=bank, wi=wi, th=th:
                             e.tensor_reduce(
                                 out=b_jnkV[:, 1 + 4 * wi + th:2 + 4 * wi + th]
                                 if False else
                                 b_jnkV[:, 1 + 2 * wi + th:2 + 2 * wi + th],
                                 in_=bank[:, 0:512], axis=AX, op=OP.max,
                                 apply_absolute_value=True), sig=V)
                mx_tok = V.n
                for b4 in range(4):
                    gen_free[pbase + b4] = [V, mx_tok]
                r.op("vector", lambda e: e.tensor_reduce(
                    out=b_jnkV[:, 9:10], in_=b_jnkV[:, 1:3], axis=AX, op=OP.max))
                r.op("vector", lambda e: e.tensor_tensor(
                    out=b_mxp[:, 17:18], in0=b_mxp[:, 17:18],
                    in1=b_jnkV[:, 9:10], op=OP.max))
                r.op("vector", lambda e: e.tensor_reduce(
                    out=b_jnkV[:, 9:10], in_=b_jnkV[:, 3:5], axis=AX, op=OP.max))
                r.op("vector", lambda e: e.tensor_tensor(
                    out=b_mxp[:, 18:19], in0=b_mxp[:, 18:19],
                    in1=b_jnkV[:, 9:10], op=OP.max), sig=V)
                gen += 1
            if oc + 2 < n_oc:
                oc2 = oc + 2
                sc_, lo = oc2 // 3, (oc2 % 3) * 128
                r.wait("sync", T, last_mm_tok[dbuf])
                for wi, woff in ((0, 0), (1, D * OSH)):
                    r.op("sync", lambda e, sc_=sc_, lo=lo, woff=woff, wi=wi,
                         dbuf=dbuf: e.dma_start(
                             out=_wst(dbuf, wi)
                             .rearrange("p (c x) -> p c x", x=128),
                             in_=d_ago[sc_, woff:woff + D * OSH]
                             .rearrange("(i o) -> i o", o=OSH)[:, lo:lo + 128]
                             .rearrange("(c p) o -> p c o", p=128)),
                         sig=P_WST[dbuf], sign=16)
                wst_tok[dbuf] = P_WST[dbuf].n
        barrier_max2(3, b_mxp[:, 17:19], n=2)
        for gslot, dst in ((3, 5), (4, 7)):
            r.op("vector", lambda e, gslot=gslot: e.tensor_tensor(
                out=b_jnkV[:, 0:1], in0=b_rs[:, gslot:gslot + 1],
                in1=b_rs[:, 1:2], op=OP.mult))
            r.op("vector", lambda e, dst=dst: e.tensor_scalar(
                out=b_rs[:, dst:dst + 1], in0=b_jnkV[:, 0:1],
                scalar1=1.0 / 32767.0, scalar2=1e-12, op0=OP.mult, op1=OP.max))
            r.op("vector", lambda e, dst=dst: e.reciprocal(
                out=b_jnkV[:, 1:2], in_=b_rs[:, dst:dst + 1]))
            r.op("vector", lambda e, dst=dst: e.tensor_tensor(
                out=b_rs[:, dst + 1:dst + 2], in0=b_rs[:, 1:2],
                in1=b_jnkV[:, 1:2], op=OP.mult), sig=V)
        return dn_tok

    def ph4():
        r.wait("scalar", T)
        r.wait("vector", T)
        r.wait("vector", S)
        r.wait("gpsimd", T)
        r.wait("gpsimd", P_UPSP)
        r.wait("gpsimd", P_UPLD)
        r.wait("vector", P_UPSP)
        r.wait("vector", P_UPLD)
        r.wait("sync", P_X1LD)
        r.wait("sync", P_X1ST)
        r.wait("vector", P_X1LD)
        r.wait("vector", P_X1ST)
        tokVg = [0] * NOC
        tokSs = [0] * NOC
        tokVu = [0] * NOC
        tokGt = [0] * NOC
        tokVm = [0] * NOC

        def Vg(i):
            r.op("vector", lambda e, i=i: e.tensor_scalar(
                out=g_slot(i).bitcast(I16), in0=g_slot(i),
                scalar1=b_rs[:, 8:9], scalar2=32767.0,
                op0=OP.mult, op1=OP.min), sig=V)
            tokVg[i] = V.n

        sg_tok = [0] * NOC
        tokCv = [0] * NOC
        tokT2 = [0] * NOC
        sigTmp = AR_A[:, 8192:12288].bitcast(F32)

        def Ss(i):
            p = i % 2
            r.wait("scalar", V, tokVg[i])
            if i >= 2:
                r.wait("scalar", V, tokVu[i - 2])
            r.op("scalar", lambda e, i=i, p=p: e.activation(
                out=t_sigF[p], in_=g_slot(i).bitcast(I16),
                func=AF.Sigmoid, scale=b_rs[:, 7:8]), sig=S)
            tokSs[i] = S.n

        Vg(0)
        Vg(1)
        Ss(0)
        for i in range(NOC):
            p = i % 2
            r.wait("vector", S, tokSs[i])
            if i >= 1:
                r.wait("vector", S, tokCv[i - 1])
            r.op("vector", lambda e, p=p: e.tensor_scalar(
                out=sigTmp, in0=t_sigF[p], scalar1=65536.0,
                scalar2=65535.0, op0=OP.mult, op1=OP.min), sig=V)
            tokVu[i] = V.n
            if i + 1 < NOC:
                Ss(i + 1)
            r.wait("scalar", V, tokVu[i])
            if i >= 2:
                r.wait("scalar", V, tokT2[i - 2])
                r.wait("scalar", (P_X1LD, P_X1ST)[p], sg_tok[i - 2])
            r.op("scalar", lambda e, p=p: e.copy(
                out=t_sigU[p], in_=sigTmp), sig=S)
            tokCv[i] = S.n
            spool = (P_X1LD, P_X1ST)[p]
            r.wait("sync", S, tokCv[i])
            r.op("sync", lambda e, i=i, p=p: e.dma_start(
                out=d_sgu[i], in_=t_sigU[p]), sig=spool, sign=16)
            sg_tok[i] = spool.n
            if i + 2 < NOC:
                Vg(i + 2)
            if i >= 1:
                r.op("vector", lambda e, i=i: e.tensor_reduce(
                    out=b_jnkV[:, i - 1:i], in_=t_prod[(i - 1) % 2], axis=AX,
                    op=OP.max, apply_absolute_value=True), sig=V)
                tokVm[i - 1] = V.n
            r.wait("vector", S, tokCv[i])
            r.op("vector", lambda e, i=i, p=p: e.tensor_tensor(
                out=t_prod[p], in0=g_slot(i).bitcast(I16), in1=t_sigU[p],
                op=OP.mult), sig=V)
            tokT2[i] = V.n
        r.op("vector", lambda e: e.tensor_reduce(
            out=b_jnkV[:, NOC - 1:NOC], in_=t_prod[(NOC - 1) % 2], axis=AX,
            op=OP.max, apply_absolute_value=True))
        r.op("vector", lambda e: e.tensor_reduce(
            out=b_mxp[:, 17:18], in_=b_jnkV[:, 0:NOC], axis=AX, op=OP.max),
            sig=V)
        barrier_max2(13, b_mxp[:, 17:18])
        r.op("vector", lambda e: e.tensor_scalar(
            out=b_jnkV[:, 0:1], in0=b_rs[:, 7:8], scalar1=1.0 / 65536.0,
            scalar2=None, op0=OP.mult))
        r.op("vector", lambda e: e.tensor_tensor(
            out=b_jnkV[:, 1:2], in0=b_jnkV[:, 0:1], in1=b_rs[:, 13:14],
            op=OP.mult))
        r.op("vector", lambda e: e.tensor_scalar(
            out=b_rs[:, 9:10], in0=b_jnkV[:, 1:2], scalar1=1.0 / 32767.0,
            scalar2=1e-12, op0=OP.mult, op1=OP.max))
        r.op("vector", lambda e: e.reciprocal(
            out=b_jnkV[:, 2:3], in_=b_rs[:, 9:10]))
        r.op("vector", lambda e: e.tensor_tensor(
            out=b_rs[:, 10:11], in0=b_jnkV[:, 0:1], in1=b_jnkV[:, 2:3],
            op=OP.mult), sig=V)

    def ph5():
        r.wait("scalar", V)
        r.wait("gpsimd", V)
        r.wait("sync", T)
        r.wait("sync", P_UPSP)
        r.wait("vector", P_UPSP)
        r.wait("vector", P_UPLD)
        r.wait("gpsimd", P_UPLD)
        r.wait("scalar", P_X1LD)
        r.wait("scalar", P_X1ST)
        r.wait("gpsimd", P_X1LD)
        r.wait("gpsimd", P_X1ST)
        tokGt2 = [0] * NOC
        tokVg2 = [0] * NOC
        tokVc = [0] * NOC
        tokGt3 = [(V, 0)] * NOC
        tokVm = [0] * NOC
        ld_tok = [0] * NOC
        prod4 = [t_prod[0], t_prod[1], t_sigF[0], t_sigF[1]]

        ld_pools5 = ((P_UPSP, P_UPLD), (P_GLD, P_GLD2))

        def Ld(i):
            eng = ("sync", "gpsimd")[i % 2]
            if i >= 2:
                sem, tok = tokGt3[i - 2]
                r.wait(eng, sem, tok)
            toks = []
            for hf in range(2):
                pool = ld_pools5[i % 2][hf]
                r.op(eng, lambda e, i=i, hf=hf: e.dma_start(
                    out=t_upb[i % 2][:, 1024 * hf:1024 * (hf + 1)],
                    in_=d_upsp[i][:, 1024 * hf:1024 * (hf + 1)]),
                    sig=pool, sign=16)
                toks.append(pool.n)
            ld_tok[i] = toks

        cv2 = [0] * NOC

        def Vg2(i):
            r.op("vector", lambda e, i=i: e.tensor_scalar(
                out=prod4[i % 4], in0=prod4[i % 4],
                scalar1=b_rs[:, 10:11], scalar2=32767.0,
                op0=OP.mult, op1=OP.min), sig=V)
            tokVg2[i] = V.n
            r.wait("scalar", V, tokVg2[i])
            r.op("scalar", lambda e, i=i: e.copy(
                out=g_slot(i).bitcast(I16), in_=prod4[i % 4]), sig=S)
            cv2[i] = S.n

        def Gt3(i):
            r.wait("vector", S, cv2[i])
            r.op("vector", lambda e, i=i: e.tensor_tensor(
                out=prod4[i % 4], in0=g_slot(i).bitcast(I16),
                in1=t_upb[i % 2].bitcast(I16), op=OP.mult), sig=V)
            tokGt3[i] = (V, V.n)

        def Vm(i):
            r.op("vector", lambda e, i=i: e.tensor_reduce(
                out=b_jnkV[:, i:i + 1], in_=prod4[i % 4], axis=AX,
                op=OP.max, apply_absolute_value=True), sig=V)
            tokVm[i] = V.n

        sgld_tok = [0] * NOC

        def LdSg(i):
            spool = (P_X1LD, P_X1ST)[i % 2]
            if i >= 2:
                r.wait("scalar", V, tokGt2[i - 2])
            r.op("scalar", lambda e, i=i: e.dma_start(
                out=t_sigU[i % 2], in_=d_sgu[i]), sig=spool, sign=16,
                dma=True)
            sgld_tok[i] = spool.n

        Ld(0)
        Ld(1)
        LdSg(0)
        LdSg(1)
        for i in range(NOC):
            p = i % 2
            if i >= 2:
                Vm(i - 2)
            if i >= 1 and i + 1 < NOC:
                LdSg(i + 1)
            for hf in range(2):
                r.wait("vector", ld_pools5[p][hf], ld_tok[i][hf])
                r.op("vector", lambda e, p=p, hf=hf: e.tensor_scalar(
                    out=t_upb[p].bitcast(I16)[:, 1024 * hf:1024 * (hf + 1)],
                    in0=t_upb[p][:, 1024 * hf:1024 * (hf + 1)],
                    scalar1=b_rs[:, 6:7], scalar2=32767.0,
                    op0=OP.mult, op1=OP.min), sig=V)
            tokVc[i] = V.n
            r.wait("vector", (P_X1LD, P_X1ST)[p], sgld_tok[i])
            r.op("vector", lambda e, i=i, p=p: e.tensor_tensor(
                out=prod4[i % 4], in0=g_slot(i).bitcast(I16), in1=t_sigU[p],
                op=OP.mult), sig=V)
            tokGt2[i] = V.n
            Vg2(i)
            if i >= 1:
                Gt3(i - 1)
                if i + 1 < NOC:
                    Ld(i + 1)
        Gt3(NOC - 1)
        Vm(NOC - 2)
        Vm(NOC - 1)
        r.op("vector", lambda e: e.tensor_reduce(
            out=b_mxp[:, 18:19], in_=b_jnkV[:, 0:NOC], axis=AX, op=OP.max),
            sig=V)
        barrier_max2(14, b_mxp[:, 18:19])
        r.op("vector", lambda e: e.tensor_tensor(
            out=b_jnkV[:, 0:1], in0=b_rs[:, 9:10], in1=b_rs[:, 5:6],
            op=OP.mult))
        r.op("vector", lambda e: e.tensor_tensor(
            out=b_jnkV[:, 1:2], in0=b_jnkV[:, 0:1], in1=b_rs[:, 14:15],
            op=OP.mult))
        r.op("vector", lambda e: e.tensor_scalar(
            out=b_rs[:, 11:12], in0=b_jnkV[:, 1:2], scalar1=1.0 / 32767.0,
            scalar2=1e-12, op0=OP.mult, op1=OP.max))
        r.op("vector", lambda e: e.reciprocal(
            out=b_jnkV[:, 2:3], in_=b_rs[:, 11:12]))
        r.op("vector", lambda e: e.tensor_tensor(
            out=b_rs[:, 12:13], in0=b_jnkV[:, 0:1], in1=b_jnkV[:, 2:3],
            op=OP.mult), sig=V)

    def ph6(dn_tok):
        r.wait("scalar", V)
        r.wait("scalar", G)
        r.wait("gpsimd", V)
        r.wait("sync", V)
        r.wait("vector", P_X1LD)
        r.wait("vector", P_X1ST)
        r.wait("vector", P_OUT)
        r.wait("vector", P_X00)
        r.wait("vector", P_UPSP)
        r.wait("vector", P_UPLD)
        r.wait("tensor", P_DN, dn_tok)
        nj = 4 * NOC
        tokLd = [0] * nj
        tokVc = [0] * nj
        tokG = [0] * nj
        tokVo = [0] * nj
        tokSc = [0] * nj
        scq_tok = [0] * 4
        x1_tok = [0] * NTILE
        ev_tok = [0] * NTILE
        out_tok = [0] * NTILE
        x1_emitted = [False] * NTILE
        bank_tok = {b: [V, V.n] for b in range(4)}

        ld_pools = (P_UPSP, P_GLD, P_UPLD, P_GLD2)
        ld_engs = ("sync", "gpsimd", "sync", "gpsimd")

        def Ld6(j):
            pool = ld_pools[j % 4]
            eng = ld_engs[j % 4]
            q, i = divmod(j, NOC)
            if j >= 4:
                r.wait(eng, G, tokG[j - 4])
            r.op(eng, lambda e, q=q, i=i, j=j: e.dma_start(
                out=t_u4[j % 4], in_=d_upsp[i][:, 512 * q:512 * (q + 1)]),
                sig=pool, sign=16)
            tokLd[j] = pool.n

        def LdX1(t):
            pool = (P_X1LD, P_X1ST)[t % 2]
            if t >= 2:
                r.wait("scalar", V, ev_tok[t - 2])
            r.op("scalar", lambda e, t=t: e.dma_start(
                out=x1b[t % 2], in_=d_x1sp[128 * t:128 * (t + 1), :]),
                sig=pool, sign=16, dma=True)
            x1_tok[t] = pool.n
            x1_emitted[t] = True

        def emit_ew(q, i):
            j = q * NOC + i
            b4 = j % 4
            if j + 3 < nj:
                Ld6(j + 3)
            pool = ld_pools[j % 4]
            r.wait("vector", pool, tokLd[j])
            r.op("vector", lambda e, b4=b4: e.tensor_scalar(
                out=t_u4[b4].bitcast(I16), in0=t_u4[b4],
                scalar1=b_rs[:, 6:7], scalar2=32767.0,
                op0=OP.mult, op1=OP.min), sig=V)
            tokVc[j] = V.n
            r.wait("gpsimd", V, tokVc[j])
            if j >= 4:
                r.wait("gpsimd", V, tokVo[j - 4])
            r.op("gpsimd", lambda e, q=q, i=i, b4=b4: e.tensor_tensor(
                out=t_p4[b4],
                in0=g_slot(i).bitcast(I16)[:, 512 * q:512 * (q + 1)],
                in1=t_u4[b4].bitcast(I16), op=OP.mult), sig=G)
            tokG[j] = G.n
            r.wait("vector", G, tokG[j])
            if j >= 4:
                r.wait("vector", S, tokSc[j - 4])
            r.op("vector", lambda e, b4=b4: e.tensor_scalar(
                out=t_oI[:, 512 * b4:512 * (b4 + 1)], in0=t_p4[b4],
                scalar1=b_rs[:, 12:13], scalar2=32767.0,
                op0=OP.mult, op1=OP.min), sig=V)
            tokVo[j] = V.n
            r.wait("scalar", V, tokVo[j])
            r.op("scalar", lambda e, q=q, i=i, b4=b4: e.copy(
                out=g_slot(i)[:, 512 * q:512 * (q + 1)],
                in_=t_oI[:, 512 * b4:512 * (b4 + 1)]), sig=S)
            tokSc[j] = S.n

        mm_tok = [0] * NTILE

        def emit_mm(t):
            r.wait("tensor", S, scq_tok[t // 4])
            for o2h in range(2):
                b = (2 * t + o2h) % 4
                sem, tok = bank_tok[b]
                r.wait("tensor", sem, tok)
                for oc in range(NOC):
                    mm(ps[b][:, 0:512],
                       g_slot(oc)[:, 128 * t:128 * (t + 1)],
                       w_dnT[:, D * oc + 512 * o2h:D * oc + 512 * (o2h + 1)],
                       oc == 0, oc == NOC - 1,
                       sig=T if oc == NOC - 1 else None)
            mm_tok[t] = T.n

        def emit_fin(t):
            r.wait("vector", T, mm_tok[t])
            xpool = (P_X1LD, P_X1ST)[t % 2]
            r.wait("vector", xpool, x1_tok[t])
            opool = (P_OUT, P_X00)[t % 2]
            if t >= 2:
                r.wait("vector", opool, out_tok[t - 2])
            for o2h in range(2):
                b = (2 * t + o2h) % 4
                r.op("vector", lambda e, b=b, o2h=o2h, t=t:
                     e.scalar_tensor_tensor(
                         out=xrb[t % 2][:, 512 * o2h:512 * (o2h + 1)],
                         in0=ps[b][:, 0:512], scalar=b_rs[:, 11:12],
                         in1=x1b[t % 2][:, 512 * o2h:512 * (o2h + 1)],
                         op0=OP.mult, op1=OP.add), sig=V if o2h == 1 else None)
            ev_tok[t] = V.n
            for o2h in range(2):
                bank_tok[(2 * t + o2h) % 4] = [V, ev_tok[t]]
            r.wait("sync", V, ev_tok[t])
            r.op("sync", lambda e, t=t: e.dma_start(
                out=d_out[128 * t:128 * (t + 1), :], in_=xrb[t % 2]),
                sig=opool, sign=16)
            out_tok[t] = opool.n
            if t + 2 < NTILE:
                LdX1(t + 2)

        for j in range(3):
            Ld6(j)
        LdX1(0)
        LdX1(1)
        pend = []
        for i in range(NOC):
            emit_ew(0, i)
        scq_tok[0] = S.n
        for q in range(1, 5):
            for i in range(NOC):
                if q < 4:
                    emit_ew(q, i)
                if i % 6 == 5:
                    t = 4 * (q - 1) + i // 6
                    emit_mm(t)
                    if pend:
                        emit_fin(pend.pop(0))
                    pend.append(t)
            if q < 4:
                scq_tok[q] = S.n
        for t in pend:
            emit_fin(t)
        r.wait("sync", P_OUT)
        r.wait("sync", P_X00)
        r.wait("tensor", P_OUT)
        r.wait("tensor", P_X00)

    phmax = int(os.environ.get("BASSMK_PHASES", "9"))
    nseg = int(os.environ.get("BASSMK_SEGS", str(NSEG)))
    with es:
        block = es.enter_context(nc.Block())
        ph0()
        if phmax >= 1:
            for seg in range(nseg):
                if seg == 0:
                    fence()
                attn_segment(seg, seg + 1 < nseg)
        if phmax >= 2 and nseg == NSEG:
            ph2()
        if phmax >= 3 and nseg == NSEG:
            dn_tok = ph3()
            if phmax >= 4:
                ph4()
            if phmax >= 5:
                ph5()
            if phmax >= 6:
                ph6(dn_tok)
        fence()
        r.flush(block)
    return nc


_NC_CACHE = None


def kernel(**inputs):
    global _NC_CACHE
    in_maps = _host_prepare(inputs)
    if _NC_CACHE is None:
        _NC_CACHE = _build()
    res = run_bass_kernel_spmd(_NC_CACHE, in_maps, core_ids=list(range(8)))
    out = np.stack([rr["out"] for rr in res.results], 0)
    return out.reshape(8, 2048, 1024).astype(np.float32)


# version 14
# speedup vs baseline: 1.0013x; 1.0013x over previous
import contextlib
import os

import numpy as np
import ml_dtypes

import concourse.bass as bass
import concourse.mybir as mybir
from concourse.bass_utils import run_bass_kernel_spmd
from concourse.alu_op_type import AluOpType as OP

F32 = mybir.dt.float32
BF16 = mybir.dt.bfloat16
I16 = mybir.dt.int16
I32 = mybir.dt.int32
U16 = mybir.dt.uint16
AF = mybir.ActivationFunctionType
AX = mybir.AxisListType.X
BF = ml_dtypes.bfloat16

L = 2048
D = 1024
H = 4
SEG = 256
NSEG = L // SEG
NTT = SEG // 128
NTILE = L // 128
DI = 3072
NOC = DI // 128
OSH = DI // 8
AGW = 2 * D * OSH + DI * 128

_ENGS = ("sync", "tensor", "vector", "scalar", "gpsimd")


class Sem:
    __slots__ = ("h", "n", "name")

    def __init__(self, h, name):
        self.h, self.n, self.name = h, 0, name


class Rec:
    def __init__(self):
        self.ops = {e: [] for e in _ENGS}
        self.auto = {}
        self.hiwater = {}

    def op(self, eng, build, sig=None, sign=1, dma=False):
        auto = None if dma else self.auto.get(eng)
        if auto is not None:
            if sig is None:
                sig = auto
            assert sig is auto, f"{eng} ops must sig their own sem"
            if auto.n > 0:
                self.ops[eng].append(
                    lambda e, s=auto, vv=auto.n: e.wait_ge(s.h, vv))
        if sig is not None:
            sig.n += sign

            def wrapped(e, b=build, s=sig, v=sign):
                b(e).then_inc(s.h, v)
        else:
            def wrapped(e, b=build):
                b(e)
        self.ops[eng].append(wrapped)
        return sig.n if sig is not None else None

    def wait(self, eng, sem, val=None):
        v = sem.n if val is None else val
        if v <= 0:
            return
        key = (eng, sem.name)
        if v <= self.hiwater.get(key, 0):
            return
        self.hiwater[key] = v
        self.ops[eng].append(lambda e, s=sem, vv=v: e.wait_ge(s.h, vv))

    def flush(self, block):
        for eng in _ENGS:
            lst = self.ops[eng]
            if not lst:
                continue

            def body(e, lst=list(lst)):
                for f in lst:
                    f(e)
            getattr(block, eng)(body)
        self.ops = {e: [] for e in _ENGS}


def _masks_host():
    c = 32
    Lb = np.zeros((128, 128), np.float32)
    Ub = np.zeros((128, 128), np.float32)
    Ib = np.zeros((128, 128), np.float32)
    for p in range(4):
        s = slice(32 * p, 32 * p + 32)
        Lb[s, s] = np.tril(np.ones((c, c), np.float32), -1)
        Ub[s, s] = np.triu(np.ones((c, c), np.float32), 1)
        Ib[s, s] = np.eye(c, dtype=np.float32)
    CM = np.zeros((128, 128), np.float32)
    for i in range(4):
        for j in range(4):
            if i > j:
                CM[32 * i:32 * i + 32, 32 * j:32 * j + 32] = 1.0
    MT = np.zeros((128, 128), np.float32)
    for i in range(4):
        for j in range(4):
            bi, bj = slice(32 * i, 32 * i + 32), slice(32 * j, 32 * j + 32)
            if i > j:
                MT[bj, bi] = 1.0
            elif i == j:
                MT[bj, bi] = np.tril(np.ones((c, c), np.float32), 0).T
    pat = np.zeros((32, 1024), np.float32)
    for b in range(32):
        pat[b, 32 * b:32 * b + 32] = 1.0
    magic = np.full((128, 8), 0x5F3759DF, np.int32)
    return dict(
        m_negL=(-Lb).astype(BF), m_negU=(-Ub).astype(BF), m_I=Ib.astype(BF),
        m_cmLneg=(-CM).astype(BF), m_cmUpos=CM.T.copy().astype(BF),
        m_cmUneg=(-CM.T).astype(BF), m_MT=MT.astype(BF),
        identB=np.eye(128, dtype=BF), identF=np.eye(128, dtype=np.float32),
        pat=pat, magic=magic,
    )


def _host_prepare(inputs):
    hs = np.ascontiguousarray(inputs["L0_attn_hidden_states"], dtype=np.float32)
    WqT = np.ascontiguousarray(np.asarray(inputs["L0_attn_Wq"], np.float32).T)
    WkT = np.ascontiguousarray(np.asarray(inputs["L0_attn_Wk"], np.float32).T)
    WvT = np.ascontiguousarray(np.asarray(inputs["L0_attn_Wv"], np.float32).T)
    WoT = np.ascontiguousarray(np.asarray(inputs["L0_attn_Wo"], np.float32).T)
    WbmT = np.ascontiguousarray(np.concatenate(
        [np.asarray(inputs["L0_attn_Wb"], np.float32),
         np.asarray(inputs["L0_attn_Wmix"], np.float32)], 0).T)
    convw = np.ascontiguousarray(
        np.stack([np.asarray(inputs["L0_attn_conv_q"], np.float32),
                  np.asarray(inputs["L0_attn_conv_k"], np.float32),
                  np.asarray(inputs["L0_attn_conv_v"], np.float32)], 0)
        .reshape(3, 8, 128, 4).transpose(2, 0, 1, 3).reshape(128, 96))
    normw_t = np.ascontiguousarray(
        np.tile(np.asarray(inputs["L0_attn_norm_w"], np.float32), H)
        .reshape(8, 128).T)
    w_up = np.asarray(inputs["L0_ffn_w_up"], np.float32)
    w_gate = np.asarray(inputs["L0_ffn_w_gate"], np.float32)
    w_down = np.asarray(inputs["L0_ffn_w_down"], np.float32)
    masks = _masks_host()
    in_maps = []
    for c in range(8):
        in_maps.append(dict(
            x0=np.ascontiguousarray(hs[c]),
            wqt=WqT, wkt=WkT, wvt=WvT, wot=WoT, wbmt=WbmT,
            convw=convw, normwt=normw_t,
            upsh=np.ascontiguousarray(w_up[OSH * c:OSH * (c + 1)]),
            gash=np.ascontiguousarray(w_gate[OSH * c:OSH * (c + 1)]),
            dnsh=np.ascontiguousarray(w_down[128 * c:128 * (c + 1)]),
            **masks,
        ))
    return in_maps


class Carver:

    def __init__(self, arena_ap, ncols):
        self.a = arena_ap
        self.n = ncols
        self.off = 0

    def take(self, cols, dtype=BF16):
        o = self.off
        self.off += cols
        assert self.off <= self.n, f"arena overflow {self.off} > {self.n}"
        v = self.a[:, o:o + cols]
        if dtype is not BF16:
            v = v.bitcast(dtype)
        return v

    def at(self, off, cols, dtype=BF16):
        v = self.a[:, off:off + cols]
        if dtype is not BF16:
            v = v.bitcast(dtype)
        return v


def _build():
    nc = bass.Bass("TRN2", target_bir_lowering=False)
    es = contextlib.ExitStack()

    dt_in = lambda n, s, d: nc.dram_tensor(n, s, d, kind="ExternalInput").ap()
    d_x0 = dt_in("x0", [L, D], F32)
    d_wqt = dt_in("wqt", [D, D], F32)
    d_wkt = dt_in("wkt", [D, D], F32)
    d_wvt = dt_in("wvt", [D, D], F32)
    d_wot = dt_in("wot", [D, D], F32)
    d_wbmt = dt_in("wbmt", [D, 8], F32)
    d_convw = dt_in("convw", [128, 96], F32)
    d_normwt = dt_in("normwt", [128, 8], F32)
    d_upsh = dt_in("upsh", [OSH, D], F32)
    d_gash = dt_in("gash", [OSH, D], F32)
    d_dnsh = dt_in("dnsh", [128, DI], F32)
    d_cst = {}
    for k in ("m_negL", "m_negU", "m_I", "m_cmLneg", "m_cmUpos", "m_cmUneg",
              "m_MT", "identB"):
        d_cst[k] = dt_in(k, [128, 128], BF16)
    d_cst["identF"] = dt_in("identF", [128, 128], F32)
    d_cst["pat"] = dt_in("pat", [32, 1024], F32)
    d_cst["magic"] = dt_in("magic", [128, 8], I32)
    d_out = nc.dram_tensor("out", [L, D], F32, kind="ExternalOutput").ap()

    _dbg = os.environ.get("BASSMK_DEBUG", "") == "1"
    d_x1sp = nc.dram_tensor("x1sp", [L, D], F32,
                            kind="ExternalOutput" if _dbg else "Internal").ap()
    d_upsp = nc.dram_tensor("upsp", [NOC, 128, L], BF16,
                            kind="ExternalOutput" if _dbg else "Internal").ap()
    d_sgu = nc.dram_tensor("sgu", [NOC, 128, L], U16).ap()
    d_agi = nc.dram_tensor("agi", [AGW], BF16).ap()
    d_ago = nc.dram_tensor("ago", [8, AGW], BF16).ap()
    d_mxi = nc.dram_tensor("mxi", [128, 1], F32).ap()
    d_mxo = nc.dram_tensor("mxo", [128, 1], F32).ap()
    d_mxi2 = nc.dram_tensor("mxi2", [2, 128], F32).ap()
    d_mxo2 = nc.dram_tensor("mxo2", [2, 128], F32).ap()
    d_sc = nc.dram_tensor("dsc", [1, 16], F32,
                          kind="ExternalOutput" if _dbg else "Internal").ap()
    d_ago_dbg = _dbg

    sb = lambda n, s, d: es.enter_context(nc.sbuf_tensor(n, s, d))

    c_idB = sb("c_idB", [128, 128], BF16)
    c_idF = sb("c_idF", [128, 128], F32)
    c_mL = sb("c_mL", [128, 128], BF16)
    c_mU = sb("c_mU", [128, 128], BF16)
    c_mI = sb("c_mI", [128, 128], BF16)
    c_cmLn = sb("c_cmLn", [128, 128], BF16)
    c_cmUp = sb("c_cmUp", [128, 128], BF16)
    c_cmUn = sb("c_cmUn", [128, 128], BF16)
    c_MT = sb("c_MT", [128, 128], BF16)
    c_convw = sb("c_convw", [128, 96], F32)
    c_normwt = sb("c_normwt", [128, 8], F32)
    c_magic = sb("c_magic", [128, 8], I32)

    AR_A = sb("AR_A", [128, 12288], BF16)
    AR_W = sb("AR_W", [128, 3 * 8192], BF16)
    AR_E = sb("AR_E", [128, 49152], BF16)
    w_oT = sb("w_oT", [128, 8192], BF16)
    w_bmT = sb("w_bmT", [128, 64], BF16)

    b_x1 = sb("b_x1", [128, 1024], F32)
    b_xr = sb("b_xr", [128, 1024], F32)
    b_acc = sb("b_acc", [128, 48], F32)
    b_rn = sb("b_rn", [128, 32], F32)
    b_bm = sb("b_bm", [128, NTT * 8], F32)
    b_bx = sb("b_bx", [128, NTT * 8], F32)
    b_ln2 = sb("b_ln2", [128, 32], F32)
    b_mxp = sb("b_mxp", [128, 20], F32)
    b_rs = sb("b_rs", [128, 16], F32)
    b_sc1 = sb("b_sc1", [2, 16], F32)
    b_row = sb("b_row", [2, 128], F32)
    b_wst = sb("b_wst", [128, 4096], BF16)
    b_i32 = sb("b_i32", [128, 8], I32)
    b_jnkV = sb("b_jnkV", [128, 32], F32)

    c_diag = AR_A[:, 0:12288]
    v_fst = AR_A[:, 0:2048]
    xqT_A = AR_A[:, 2048:12288]
    xqT_B = w_oT[:, 0:6144]
    t_prod = [AR_A[:, 4096 * p:4096 * (p + 1)].bitcast(F32) for p in range(2)]
    t_upb = [AR_A[:, 8192 + 2048 * p:8192 + 2048 * (p + 1)] for p in range(2)]
    t_sigF = [w_oT[:, 4096 * p:4096 * (p + 1)].bitcast(F32) for p in range(2)]
    t_sigU = [b_wst[:, 2048 * p:2048 * (p + 1)].bitcast(U16) for p in range(2)]
    t_oI = w_oT[:].bitcast(I16)
    t_u4 = [AR_A[:, 8192 + 512 * j:8192 + 512 * (j + 1)] for j in range(4)]
    t_p4 = [AR_A[:, 1024 * j:1024 * (j + 1)].bitcast(F32) for j in range(4)]
    x1b = [b_x1[:], b_wst[:, 0:2048].bitcast(F32)]
    xrb = [b_xr[:], b_wst[:, 2048:4096].bitcast(F32)]

    def xq_sl(it, c0, cn):
        if it < 5:
            return xqT_A[:, it * 2048 + c0:it * 2048 + c0 + cn]
        return xqT_B[:, (it - 5) * 2048 + c0:(it - 5) * 2048 + c0 + cn]

    w_qT = AR_W[:, 0:8192]
    w_kT = AR_W[:, 8192:16384]
    w_vT = AR_W[:, 16384:24576]
    w_dnT = AR_W[:]

    ce = Carver(AR_E[:], 49152)
    b_xT = ce.take(2048)
    b_praw = ce.take(3 * 8 * 260)
    b_qf = ce.take(2048)
    b_ksil = ce.take(2048)
    b_vsil = ce.take(2048)
    b_khf = ce.take(2048)
    b_kbf = ce.take(2048)
    b_ktok = ce.take(2048)
    b_kbtok = ce.take(2048)
    b_vtok = ce.take(2048)
    b_vbtok = ce.take(2048)
    b_onrm = ce.take(2048)
    b_xln = ce.take(1024)
    b_x0t = ce.take(4096, F32)
    b_o2 = ce.take(2048, F32)
    b_oT = ce.take(1024)
    b_jnkS = ce.take(2048, F32)
    b_Pp = [ce.take(256) for _ in range(H)]
    b_Qp = [ce.take(256) for _ in range(H)]
    b_Mp = [ce.take(256) for _ in range(H)]
    b_nw = [ce.take(256) for _ in range(H)]
    b_U = [ce.take(256) for _ in range(H)]
    b_Lt = [ce.take(128) for _ in range(H)]
    b_LtTp = [ce.take(128) for _ in range(H)]
    b_LtTn = [ce.take(128) for _ in range(H)]
    b_L2T = [ce.take(128) for _ in range(H)]
    b_GT = [ce.take(128) for _ in range(H)]
    b_S = [ce.take(512) for _ in range(H)]
    p0 = Carver(AR_E[:], 49152)
    b_shard = p0.take(6144, F32)
    b_shq = p0.take(3072, I16)
    b_shb = p0.take(3072)
    b_sst = p0.take(192, F32)
    b_sin = p0.take(192, F32)
    b_stg = p0.take(1024)
    b_sexp_host = p0.take(1536, F32)
    b_sexp = b_sexp_host[0:32, 0:768]
    c_pat_host = p0.take(2048, F32)
    c_pat = c_pat_host[0:32, 0:1024]
    g_slot = lambda oc: AR_E[:, 2048 * oc:2048 * (oc + 1)]

    ps = [es.enter_context(nc.psum_tensor(f"ps{i}", [128, 512], F32))
          for i in range(8)]

    def pb(i):
        return ps[i][:].bitcast(BF16)

    sm = {k: Sem(es.enter_context(nc.semaphore(k)), k) for k in
          ("t", "v", "s", "cc",
           "cst", "shard", "x00", "x01", "x1st", "x1ld", "upsp", "upld",
           "wst0", "wst1", "dnasm", "bar", "out", "wgt", "agwr", "gld",
           "gld2")}
    T, V, S, CC = (sm[k] for k in ("t", "v", "s", "cc"))
    G = Sem(es.enter_context(nc.semaphore("g")), "g")
    P_CST, P_SHARD, P_X00, P_X01 = (sm[k] for k in ("cst", "shard", "x00", "x01"))
    P_X1ST, P_X1LD, P_UPSP, P_UPLD = (sm[k] for k in ("x1st", "x1ld", "upsp", "upld"))
    P_WST = (sm["wst0"], sm["wst1"])
    P_DN, P_BAR, P_OUT, P_WGT, P_AGWR = (sm[k] for k in
                                         ("dnasm", "bar", "out", "wgt", "agwr"))
    P_GLD = sm["gld"]
    P_GLD2 = sm["gld2"]

    r = Rec()
    r.auto = {"vector": V, "scalar": S}

    def mm(outap, lhsT, rhs, start, stop, sig=None, sgc=False):
        r.op("tensor", lambda e, o=outap, l=lhsT, rr=rhs, a=start, b=stop,
             g=sgc: e.matmul(out=o, lhsT=l, rhs=rr, start=a, stop=b,
                             skip_group_check=g), sig=sig)

    def tr(outap, inap, ident, sig=None):
        r.op("tensor", lambda e, o=outap, i=inap, d=ident:
             e.transpose(out=o, in_=i, identity=d), sig=sig)

    bankfree = [[V, 0] for _ in range(8)]

    def bwait(b):
        sem, tok = bankfree[b]
        r.wait("tensor", sem, tok)

    def brel(b, sem):
        bankfree[b] = [sem, sem.n]

    def rsqrt(dst, src, eps, n, scale=None):
        y = b_jnkV[:, 0:n]
        if scale is None:
            r.op("vector", lambda e, y=y, src=src: e.tensor_scalar(
                out=y, in0=src, scalar1=float(eps), scalar2=None, op0=OP.add))
        else:
            r.op("vector", lambda e, y=y, src=src: e.tensor_scalar(
                out=y, in0=src, scalar1=float(scale), scalar2=float(eps),
                op0=OP.mult, op1=OP.add))
        r.op("vector", lambda e, y=y, n=n: e.tensor_scalar(
            out=b_i32[:, 0:n], in0=y.bitcast(I32), scalar1=1, scalar2=None,
            op0=OP.arith_shift_right))
        r.op("vector", lambda e, dst=dst, n=n: e.tensor_tensor(
            out=dst.bitcast(I32), in0=c_magic[:, 0:n], in1=b_i32[:, 0:n],
            op=OP.subtract))
        t_ = b_jnkV[:, 8:8 + n]
        for _ in range(3):
            r.op("vector", lambda e, t_=t_, dst=dst: e.tensor_tensor(
                out=t_, in0=dst, in1=dst, op=OP.mult))
            r.op("vector", lambda e, t_=t_, y=y: e.tensor_tensor(
                out=t_, in0=y, in1=t_, op=OP.mult))
            r.op("vector", lambda e, t_=t_: e.tensor_scalar(
                out=t_, in0=t_, scalar1=-0.5, scalar2=1.5,
                op0=OP.mult, op1=OP.add))
            r.op("vector", lambda e, t_=t_, dst=dst: e.tensor_tensor(
                out=dst, in0=dst, in1=t_, op=OP.mult))

    def ph0():
        for key, tgt in (("m_negL", c_mL), ("m_negU", c_mU), ("m_I", c_mI),
                         ("m_cmLneg", c_cmLn), ("m_cmUpos", c_cmUp),
                         ("m_cmUneg", c_cmUn), ("m_MT", c_MT),
                         ("identB", c_idB), ("identF", c_idF),
                         ("magic", c_magic)):
            r.op("sync", lambda e, a=d_cst[key], b=tgt: e.dma_start(
                out=b[:], in_=a), sig=P_CST, sign=16)
        r.op("sync", lambda e: e.dma_start(out=c_pat, in_=d_cst["pat"]), sig=P_CST, sign=16)
        r.op("sync", lambda e: e.dma_start(out=c_convw[:], in_=d_convw), sig=P_CST, sign=16)
        r.op("sync", lambda e: e.dma_start(out=c_normwt[:], in_=d_normwt), sig=P_CST, sign=16)
        cst_tok = P_CST.n
        for dd, tt_, w in ((d_wqt, w_qT, D), (d_wkt, w_kT, D), (d_wvt, w_vT, D),
                           (d_wot, w_oT[:, :], D), (d_wbmt, w_bmT[:], 8)):
            r.op("gpsimd", lambda e, a=dd, b=tt_, w=w: e.dma_start(
                out=b.rearrange("p (it o) -> p it o", o=w),
                in_=a.rearrange("(it p) o -> p it o", p=128)), sig=P_WGT, sign=16)
        gw_tok = P_WGT.n

        r.wait("tensor", P_CST, cst_tok)
        for name, ag_off in (("up", 0), ("gate", D * OSH), ("down", 2 * D * OSH)):
            if name == "up":
                r.op("sync", lambda e: e.dma_start(
                    out=b_shard.rearrange("p (t d) -> p t d", t=3),
                    in_=d_upsh.rearrange("(t p) d -> p t d", p=128)), sig=P_SHARD, sign=16)
            elif name == "gate":
                r.wait("sync", V)
                r.wait("sync", P_AGWR)
                r.op("sync", lambda e: e.dma_start(
                    out=b_shard.rearrange("p (t d) -> p t d", t=3),
                    in_=d_gash.rearrange("(t p) d -> p t d", p=128)), sig=P_SHARD, sign=16)
            else:
                r.wait("sync", V)
                r.wait("sync", P_AGWR)
                r.op("sync", lambda e: e.dma_start(
                    out=b_shard, in_=d_dnsh), sig=P_SHARD, sign=16)
            r.wait("vector", P_SHARD)
            r.op("vector", lambda e: e.tensor_reduce(
                out=b_sst[:, 0:96],
                in_=b_shard.rearrange("p (b i) -> p b i", i=32),
                axis=AX, op=OP.max, apply_absolute_value=True))
            r.op("vector", lambda e: e.tensor_scalar(
                out=b_sst[:, 0:96], in0=b_sst[:, 0:96],
                scalar1=1.0 / 7.0, scalar2=1e-12, op0=OP.mult, op1=OP.max))
            r.op("vector", lambda e: e.reciprocal(
                out=b_sin[:, 0:96], in_=b_sst[:, 0:96]), sig=V)
            r.wait("tensor", V)
            for w_i, buf in ((0, b_sin), (1, b_sst)):
                for g in range(3):
                    tr(ps[4][0:32, 0:128], buf[:, 32 * g:32 * (g + 1)], c_idF[:],
                       sig=T)
                    r.wait("vector", T)
                    r.op("vector", lambda e, w_i=w_i, g=g: e.tensor_copy(
                        out=b_sexp[0:32, (w_i * 3 + g) * 128:(w_i * 3 + g + 1) * 128],
                        in_=ps[4][0:32, 0:128]), sig=V)
                    r.wait("tensor", V)
            for it in range(3):
                for half in range(2):
                    mm(ps[5][:, 0:512], b_sexp[0:32, it * 128:(it + 1) * 128],
                       c_pat[0:32, 512 * half:512 * (half + 1)], True, True)
                    mm(ps[6][:, 0:512], b_sexp[0:32, (3 + it) * 128:(4 + it) * 128],
                       c_pat[0:32, 512 * half:512 * (half + 1)], True, True, sig=T)
                    r.wait("vector", T)
                    sl = slice(D * it + 512 * half, D * it + 512 * (half + 1))
                    r.op("vector", lambda e, sl=sl: e.tensor_tensor(
                        out=b_shq[:, sl], in0=b_shard[:, sl], in1=ps[5][:, 0:512],
                        op=OP.mult))
                    r.op("vector", lambda e, sl=sl: e.tensor_tensor(
                        out=b_shb[:, sl], in0=b_shq[:, sl], in1=ps[6][:, 0:512],
                        op=OP.mult), sig=V)
                    r.wait("tensor", V)
            for it in range(3):
                r.wait("vector", P_AGWR)
                for ch in range(8):
                    tr(pb(7)[0:128, 0:128],
                       b_shb[:, D * it + 128 * ch:D * it + 128 * (ch + 1)],
                       c_idB[:], sig=T)
                    r.wait("vector", T)
                    r.op("vector", lambda e, ch=ch: e.tensor_copy(
                        out=b_stg[:, 128 * ch:128 * (ch + 1)],
                        in_=pb(7)[0:128, 0:128]), sig=V)
                    r.wait("tensor", V)
                r.wait("gpsimd", V)
                if name != "down":
                    r.op("gpsimd", lambda e, it=it, off=ag_off: e.dma_start(
                        out=d_agi[off:off + D * OSH]
                        .rearrange("(i o) -> i o", o=OSH)[:, 128 * it:128 * (it + 1)]
                        .rearrange("(c p) o -> p c o", p=128),
                        in_=b_stg.rearrange("p (c q) -> p c q", q=128)),
                        sig=P_AGWR, sign=16)
                else:
                    r.op("gpsimd", lambda e, it=it, off=ag_off: e.dma_start(
                        out=d_agi[off + it * D * 128:off + (it + 1) * D * 128]
                        .rearrange("(i o) -> i o", o=128)
                        .rearrange("(c p) o -> p c o", p=128),
                        in_=b_stg.rearrange("p (c q) -> p c q", q=128)),
                        sig=P_AGWR, sign=16)
        r.wait("gpsimd", P_AGWR)
        r.op("gpsimd", lambda e: e.collective_compute(
            "AllGather", OP.bypass, replica_groups=[list(range(8))],
            ins=[d_agi.tensor.ap().opt()], outs=[d_ago.tensor.ap().opt()]),
            sig=CC)

        r.wait("vector", P_CST, cst_tok)
        for col in range(96):
            r.op("vector", lambda e, c=col: e.tensor_scalar(
                out=c_diag[:, 128 * c:128 * (c + 1)], in0=c_idB[:],
                scalar1=c_convw[:, c:c + 1], scalar2=None, op0=OP.mult),
                sig=V if col == 95 else None)
        r.wait("vector", P_WGT, gw_tok)
        wov = w_oT[:].rearrange("p (it o) -> p it o", o=D)
        for it in range(8):
            r.op("vector", lambda e, it=it: e.tensor_scalar(
                out=wov[:, it, :], in0=wov[:, it, :],
                scalar1=c_normwt[:, it:it + 1], scalar2=None, op0=OP.mult),
                sig=V if it == 7 else None)

    A_state = {}
    xln_buf = [b_xln, b_o2.bitcast(BF16)[:, 0:1024]]

    def A_load(seg, tt, xgate=None):
        A_state.setdefault(seg, {"ap": [0, 0], "xt": [0, 0], "st": [0, 0]})
        t0 = seg * SEG
        px = P_X00 if tt == 0 else P_X01
        if xgate is not None:
            r.wait("sync", V, xgate)
        else:
            r.wait("sync", V)
        r.op("sync", lambda e, tt=tt, t0=t0: e.dma_start(
            out=b_x0t[:, 1024 * tt:1024 * (tt + 1)],
            in_=d_x0[t0 + 128 * tt:t0 + 128 * (tt + 1), :]), sig=px, sign=16)

    def A_stats(seg, tt):
        st = A_state[seg]
        px = P_X00 if tt == 0 else P_X01
        r.wait("scalar", px)
        r.op("scalar", lambda e, tt=tt: e.activation(
            out=b_jnkS, in_=b_x0t[:, 1024 * tt:1024 * (tt + 1)], func=AF.Copy,
            accum_out=b_acc[:, 4 * tt:4 * tt + 1]))
        r.op("scalar", lambda e, tt=tt: e.activation(
            out=b_jnkS, in_=b_x0t[:, 1024 * tt:1024 * (tt + 1)], func=AF.Square,
            accum_out=b_acc[:, 4 * tt + 1:4 * tt + 2]), sig=S)
        st["st"][tt] = S.n

    def A_load_stats(seg, tt, xgate=None):
        A_load(seg, tt, xgate)
        A_stats(seg, tt)

    def A_apply(seg, tt):
        st = A_state[seg]
        r.wait("vector", S, st["st"][tt])
        m = b_acc[:, 8 + 4 * tt:9 + 4 * tt]
        va = b_acc[:, 9 + 4 * tt:10 + 4 * tt]
        r.op("vector", lambda e, tt=tt, m=m: e.tensor_scalar(
            out=m, in0=b_acc[:, 4 * tt:4 * tt + 1], scalar1=1.0 / D,
            scalar2=None, op0=OP.mult))
        r.op("vector", lambda e, m=m: e.tensor_tensor(
            out=b_jnkV[:, 16:17], in0=m, in1=m, op=OP.mult))
        r.op("vector", lambda e, tt=tt, va=va: e.scalar_tensor_tensor(
            out=va, in0=b_acc[:, 4 * tt + 1:4 * tt + 2], scalar=1.0 / D,
            in1=b_jnkV[:, 16:17], op0=OP.mult, op1=OP.subtract))
        rr = b_acc[:, 10 + 4 * tt:11 + 4 * tt]
        rsqrt(rr, va, 1e-5, 1)
        r.op("vector", lambda e, tt=tt, m=m, rr=rr: e.tensor_scalar(
            out=xln_buf[tt], in0=b_x0t[:, 1024 * tt:1024 * (tt + 1)],
            scalar1=m, scalar2=rr, op0=OP.subtract, op1=OP.mult), sig=V)
        st["ap"][tt] = V.n

    def A_transpose(seg, tt, bank):
        st = A_state[seg]
        xtv = b_xT.rearrange("p (c x) -> p c x", x=SEG)
        bwait(bank)
        r.wait("tensor", V, st["ap"][tt])
        for ch in range(8):
            tr(pb(bank)[0:128, 128 * ch:128 * (ch + 1)],
               xln_buf[tt][:, 128 * ch:128 * (ch + 1)], c_idB[:],
               sig=T if ch == 7 else None)
        tr_tok = T.n
        r.wait("scalar", T, tr_tok)
        r.op("scalar", lambda e, tt=tt, bank=bank: e.copy(
            out=xtv[:, :, 128 * tt:128 * (tt + 1)],
            in_=pb(bank)[:, 0:1024].rearrange("p (c x) -> p c x", x=128)),
            sig=S)
        st["xt"][tt] = S.n
        brel(bank, S)

    def attn_segment(seg, has_next):
        t0 = seg * SEG
        prx = b_praw.rearrange("p (r c x) -> p r c x", r=3, c=8)
        xtv = b_xT.rearrange("p (c x) -> p c x", x=SEG)
        khfv = b_khf.rearrange("p (c x) -> p c x", x=SEG)
        kbfv = b_kbf.rearrange("p (c x) -> p c x", x=SEG)
        qfv = b_qf.rearrange("p (c x) -> p c x", x=SEG)

        if seg == 0:
            r.op("vector", lambda e: e.memset(prx[:, :, :, 0:4], 0.0), sig=V)
        else:
            r.op("vector", lambda e: e.tensor_copy(
                out=prx[:, :, :, 1:4], in_=prx[:, :, :, 257:260]), sig=V)
        prb_tok = V.n

        if seg == 0:
            for tt in range(NTT):
                A_load_stats(seg, tt)
            for tt in range(NTT):
                A_apply(seg, tt)
            A_transpose(seg, 0, 2)
            A_transpose(seg, 1, 3)
        xt_tok = A_state[seg]["xt"]

        for tt in range(NTT):
            bwait(0)
            r.wait("tensor", S, xt_tok[tt])
            for it in range(8):
                mm(ps[0][:, 0:8], xtv[:, it, 128 * tt:128 * (tt + 1)],
                   w_bmT[:].rearrange("p (c x) -> p c x", x=8)[:, it, :],
                   it == 0, it == 7, sig=T if it == 7 else None)
            r.wait("scalar", T)
            r.op("scalar", lambda e, tt=tt: e.activation(
                out=b_bm[:, 8 * tt:8 * (tt + 1)], in_=ps[0][:, 0:8],
                func=AF.Sigmoid), sig=S)
            brel(0, S)

        r.wait("tensor", V, prb_tok)
        pr_tok = [[0] * 8 for _ in range(3)]
        for pi, wT in enumerate((w_qT, w_kT, w_vT)):
            wv = wT.rearrange("p (i o) -> p i o", o=D)
            for c in range(8):
                b = c % 2
                bwait(b)
                for it in range(8):
                    mm(ps[b][:, 0:256], wv[:, it, 128 * c:128 * (c + 1)],
                       xtv[:, it, :], it == 0, it == 7,
                       sig=T if it == 7 else None)
                mm_tok = T.n
                r.wait("vector", T, mm_tok)
                r.op("vector", lambda e, pi=pi, c=c, b=b: e.tensor_copy(
                    out=prx[:, pi, c, 4:260], in_=ps[b][:, 0:256]), sig=V)
                pr_tok[pi][c] = V.n
                brel(b, V)
        for pi, dst in enumerate((b_qf, b_ksil, b_vsil)):
            dv = dst.rearrange("p (c x) -> p c x", x=SEG)
            for c in range(8):
                b = 2 + (c % 2)
                bwait(b)
                r.wait("tensor", V, pr_tok[pi][c])
                for tap in range(4):
                    col = (pi * 8 + c) * 4 + tap
                    mm(ps[b][:, 0:256], c_diag[:, 128 * col:128 * (col + 1)],
                       prx[:, pi, c, 1 + tap:257 + tap],
                       tap == 0, tap == 3, sig=T if tap == 3 else None)
                mm_tok = T.n
                r.wait("scalar", T, mm_tok)
                r.op("scalar", lambda e, dv=dv, c=c, b=b: e.activation(
                    out=dv[:, c, :], in_=ps[b][:, 0:256], func=AF.Silu), sig=S)
                brel(b, S)

        r.wait("tensor", S)
        ksv = b_ksil.rearrange("p (c x) -> p c x", x=SEG)
        vsv = b_vsil.rearrange("p (c x) -> p c x", x=SEG)
        e_bq = (2, 0)
        e_bk = (3, 1)
        sq_tok = [0, 0]
        qsq_tok = [0, 0]
        kv_tok = [0, 0]

        def E_qk(tt):
            bq, bk = e_bq[tt], e_bk[tt]
            ac = 16 + 8 * tt
            bwait(bq)
            for ch in range(8):
                tr(pb(bq)[0:128, 128 * ch:128 * (ch + 1)],
                   qfv[:, ch, 128 * tt:128 * (tt + 1)], c_idB[:],
                   sig=T if ch == 7 else None)
            qtr = T.n
            bwait(bk)
            for ch in range(8):
                tr(pb(bk)[0:128, 128 * ch:128 * (ch + 1)],
                   ksv[:, ch, 128 * tt:128 * (tt + 1)], c_idB[:],
                   sig=T if ch == 7 else None)
            ktr = T.n
            r.wait("scalar", T, qtr)
            for h in range(H):
                r.op("scalar", lambda e, h=h, bq=bq, ac=ac: e.activation(
                    out=b_jnkS[:, 0:256], in_=pb(bq)[:, 256 * h:256 * (h + 1)],
                    func=AF.Square, accum_out=b_acc[:, ac + h:ac + h + 1]),
                    sig=S if h == 3 else None)
            qsq_tok[tt] = S.n
            brel(bq, S)
            r.wait("scalar", T, ktr)
            for h in range(H):
                r.op("scalar", lambda e, h=h, bk=bk, ac=ac: e.activation(
                    out=b_jnkS[:, 0:256], in_=pb(bk)[:, 256 * h:256 * (h + 1)],
                    func=AF.Square, accum_out=b_acc[:, ac + 4 + h:ac + 5 + h]),
                    sig=S if h == 3 else None)
            sq_tok[tt] = S.n

        def E_chain(tt):
            bq, bk = e_bq[tt], e_bk[tt]
            ac = 16 + 8 * tt
            r.wait("vector", S, sq_tok[tt])
            rsqrt(b_rn[:, 16 * tt:16 * tt + 4], b_acc[:, ac:ac + 4], 1e-6, 4)
            rsqrt(b_rn[:, 16 * tt + 4:16 * tt + 8], b_acc[:, ac + 4:ac + 8],
                  1e-6, 4)
            r.op("vector", lambda e, tt=tt: e.tensor_tensor(
                out=b_bx[:, 8 * tt:8 * tt + 4],
                in0=b_rn[:, 16 * tt + 4:16 * tt + 8],
                in1=b_bm[:, 8 * tt:8 * tt + 4], op=OP.mult))
            r.op("vector", lambda e, tt=tt: e.tensor_scalar(
                out=b_bx[:, 8 * tt + 4:8 * tt + 8],
                in0=b_bm[:, 8 * tt + 4:8 * tt + 8],
                scalar1=-1.0, scalar2=1.0, op0=OP.mult, op1=OP.add))
            r.op("vector", lambda e, tt=tt: e.tensor_tensor(
                out=b_rn[:, 16 * tt + 8:16 * tt + 12],
                in0=b_rn[:, 16 * tt:16 * tt + 4],
                in1=b_bm[:, 8 * tt + 4:8 * tt + 8], op=OP.mult))
            for h in range(H):
                r.op("vector", lambda e, tt=tt, h=h, bk=bk: e.tensor_scalar(
                    out=b_ktok[:, 1024 * tt + 256 * h:1024 * tt + 256 * (h + 1)],
                    in0=pb(bk)[:, 256 * h:256 * (h + 1)],
                    scalar1=b_rn[:, 16 * tt + 4 + h:16 * tt + 5 + h],
                    scalar2=None, op0=OP.mult))
                r.op("vector", lambda e, tt=tt, h=h, bk=bk: e.tensor_scalar(
                    out=b_kbtok[:, 1024 * tt + 256 * h:1024 * tt + 256 * (h + 1)],
                    in0=pb(bk)[:, 256 * h:256 * (h + 1)],
                    scalar1=b_bx[:, 8 * tt + h:8 * tt + 1 + h],
                    scalar2=None, op0=OP.mult), sig=V if h == 3 else None)
            kv_tok[tt] = V.n
            brel(bk, V)

        def E_v_back(tt):
            bq, bk = e_bq[tt], e_bk[tt]
            bwait(bq)
            for ch in range(8):
                tr(pb(bq)[0:128, 128 * ch:128 * (ch + 1)],
                   vsv[:, ch, 128 * tt:128 * (tt + 1)], c_idB[:],
                   sig=T if ch == 7 else None)
            vtr = T.n
            r.wait("vector", T, vtr)
            r.op("vector", lambda e, tt=tt, bq=bq: e.tensor_copy(
                out=b_vtok[:, 1024 * tt:1024 * (tt + 1)], in_=pb(bq)[:, 0:1024]))
            for h in range(H):
                r.op("vector", lambda e, tt=tt, h=h, bq=bq: e.tensor_scalar(
                    out=b_vbtok[:, 1024 * tt + 256 * h:1024 * tt + 256 * (h + 1)],
                    in0=pb(bq)[:, 256 * h:256 * (h + 1)],
                    scalar1=b_bm[:, 8 * tt + h:8 * tt + 1 + h],
                    scalar2=None, op0=OP.mult), sig=V if h == 3 else None)
            brel(bq, V)
            bwait(bk)
            for src, dstf in ((b_ktok, b_khf), (b_kbtok, b_kbf)):
                for ch in range(8):
                    tr(pb(bk)[0:128, 128 * ch:128 * (ch + 1)],
                       src[:, 1024 * tt + 128 * ch:1024 * tt + 128 * (ch + 1)],
                       c_idB[:], sig=T if ch == 7 else None)
                r.wait("vector", T)
                r.op("vector", lambda e, tt=tt, dstf=dstf, bk=bk: e.tensor_copy(
                    out=dstf.rearrange("p (c x) -> p c x", x=SEG)
                    [:, :, 128 * tt:128 * (tt + 1)],
                    in_=pb(bk)[:, 0:1024].rearrange("p (c x) -> p c x", x=128)),
                    sig=V)
                r.wait("tensor", V)
            brel(bk, V)

        E_qk(0)
        E_qk(1)
        E_chain(0)
        E_v_back(0)
        E_chain(1)
        E_v_back(1)

        r.wait("tensor", V)
        lad_tok = [0] * H
        for h in range(H):
            bank = ps[h % 2]
            for blk in range(NTT):
                ts = slice(128 * blk, 128 * (blk + 1))
                bwait(h % 2)
                for ci, cc in enumerate((2 * h, 2 * h + 1)):
                    mm(bank[:, 0:128], kbfv[:, cc, ts], khfv[:, cc, ts],
                       ci == 0, ci == 1)
                for ci, cc in enumerate((2 * h, 2 * h + 1)):
                    mm(bank[:, 128:256], khfv[:, cc, ts], kbfv[:, cc, ts],
                       ci == 0, ci == 1, sig=T if ci == 1 else None)
                gr_tok = T.n
                r.wait("vector", T, gr_tok)
                bs = slice(128 * blk, 128 * (blk + 1))
                r.op("vector", lambda e, h=h, bs=bs, bank=bank: e.tensor_tensor(
                    out=b_Pp[h][:, bs], in0=bank[:, 0:128], in1=c_mL[:],
                    op=OP.mult))
                r.op("vector", lambda e, h=h, bs=bs, bank=bank: e.tensor_tensor(
                    out=b_Mp[h][:, bs], in0=bank[:, 128:256], in1=c_mU[:],
                    op=OP.mult))
                r.op("vector", lambda e, h=h, bs=bs: e.tensor_tensor(
                    out=b_Mp[h][:, bs], in0=b_Mp[h][:, bs], in1=c_mI[:],
                    op=OP.add), sig=V)
                brel(h % 2, V)
            for lev in range(4):
                r.op("vector", lambda e, h=h: e.transpose(
                    out=b_Qp[h][:, :], in_=b_Pp[h][:, :]), sig=V)
                vt_tok = V.n
                r.wait("tensor", V, vt_tok)
                bwait(h % 2)
                for blk in range(NTT):
                    bs = slice(128 * blk, 128 * (blk + 1))
                    mm(bank[:, 128 * blk:128 * (blk + 1)],
                       b_Qp[h][:, bs], b_Pp[h][:, bs], True, True,
                       sig=T if blk == NTT - 1 else None)
                sq_tok = T.n
                r.wait("scalar", T, sq_tok)
                r.op("scalar", lambda e, h=h, bank=bank: e.copy(
                    out=b_Pp[h][:, :], in_=bank[:, 0:256]), sig=S)
                pe_tok = S.n
                r.wait("tensor", S, pe_tok)
                for blk in range(NTT):
                    bs = slice(128 * blk, 128 * (blk + 1))
                    mm(bank[:, 256 + 128 * blk:384 + 128 * blk],
                       b_Pp[h][:, bs], b_Mp[h][:, bs], True, True,
                       sig=T if blk == NTT - 1 else None)
                pr_tok2 = T.n
                r.wait("vector", T, pr_tok2)
                r.op("vector", lambda e, h=h, bank=bank: e.tensor_tensor(
                    out=b_Mp[h][:, :], in0=bank[:, 256:512], in1=b_Mp[h][:, :],
                    op=OP.add), sig=V)
                brel(h % 2, V)
            lad_tok[h] = V.n

        sup_tok = [0] * H
        onrm_tok = [0] * NTT
        for blk in range(NTT):
            ts = slice(128 * blk, 128 * (blk + 1))
            first = (seg == 0 and blk == 0)
            for h in range(H):
                p = h % 2
                bankA = ps[p]
                bankU = ps[2 + p]
                r.wait("tensor", V, lad_tok[h])
                if not first:
                    r.wait("scalar", T, sup_tok[h] if sup_tok[h] else T.n)
                    r.op("scalar", lambda e, h=h: e.copy(
                        out=b_S[h][:, :], in_=ps[4 + h][:, 0:512]), sig=S)
                    sc_tok = S.n
                    r.wait("tensor", S, sc_tok)
                bwait(2 + p)
                mm(bankU[:, 0:256], b_Mp[h][:, ts],
                   b_vbtok[:, 1024 * blk + 256 * h:1024 * blk + 256 * (h + 1)],
                   True, first)
                bwait(p)
                for hf in range(2):
                    mm(bankA[:, 128 * hf:128 * (hf + 1)],
                       b_kbtok[:, 1024 * blk + 256 * h + 128 * hf:
                               1024 * blk + 256 * h + 128 * (hf + 1)],
                       b_Mp[h][:, ts], hf == 0, hf == 1,
                       sig=T if hf == 1 else None)
                wt_tok = T.n
                r.wait("vector", T, wt_tok)
                r.op("vector", lambda e, h=h, bankA=bankA: e.tensor_scalar(
                    out=b_nw[h][:, :], in0=bankA[:, 0:256],
                    scalar1=-1.0, scalar2=None, op0=OP.mult), sig=V)
                nw_tok = V.n
                r.wait("tensor", V, nw_tok)
                nwv = b_nw[h]
                if not first:
                    for hf in range(2):
                        mm(bankU[:, 0:256], nwv[:, 128 * hf:128 * (hf + 1)],
                           b_S[h][:, 256 * hf:256 * (hf + 1)], False, hf == 1)
                for hf in range(2):
                    mm(bankA[:, 256:384], nwv[:, 128 * hf:128 * (hf + 1)],
                       khfv[:, 2 * h + hf, ts], hf == 0, hf == 1)
                for hf in range(2):
                    mm(bankA[:, 384:512], khfv[:, 2 * h + hf, ts],
                       nwv[:, 128 * hf:128 * (hf + 1)], hf == 0, hf == 1,
                       sig=T if hf == 1 else None)
                lr_tok = T.n
                r.wait("vector", T, lr_tok)
                r.op("vector", lambda e, h=h, bankU=bankU: e.tensor_copy(
                    out=b_U[h][:, :], in_=bankU[:, 0:256]))
                r.op("vector", lambda e, h=h, bankA=bankA: e.tensor_tensor(
                    out=b_Lt[h][:, :], in0=bankA[:, 256:384], in1=c_cmLn[:],
                    op=OP.mult))
                r.op("vector", lambda e, h=h, bankA=bankA: e.tensor_tensor(
                    out=b_LtTp[h][:, :], in0=bankA[:, 384:512], in1=c_cmUn[:],
                    op=OP.mult))
                r.op("vector", lambda e, h=h, bankA=bankA: e.tensor_tensor(
                    out=b_LtTn[h][:, :], in0=bankA[:, 384:512], in1=c_cmUp[:],
                    op=OP.mult), sig=V)
                msk_tok = V.n
                r.wait("tensor", V, msk_tok)
                mm(bankA[:, 0:128], b_Lt[h][:, :], b_LtTp[h][:, :], True, True,
                   sig=T)
                l2_tok = T.n
                r.wait("vector", T, l2_tok)
                r.op("vector", lambda e, h=h, bankA=bankA: e.tensor_copy(
                    out=b_L2T[h][:, :], in_=bankA[:, 0:128]), sig=V)
                l2e_tok = V.n
                r.wait("tensor", V, l2e_tok)
                mm(bankU[:, 256:512], b_L2T[h][:, :], b_U[h][:, :], True, True,
                   sig=T)
                t1_tok = T.n
                r.wait("vector", T, t1_tok)
                r.op("vector", lambda e, h=h, bankU=bankU: e.tensor_tensor(
                    out=b_U[h][:, :], in0=bankU[:, 256:512], in1=b_U[h][:, :],
                    op=OP.add), sig=V)
                t1e_tok = V.n
                r.wait("tensor", V, t1e_tok)
                mm(bankU[:, 0:256], b_LtTn[h][:, :], b_U[h][:, :], True, True,
                   sig=T)
                up_tok2 = T.n
                r.wait("vector", T, up_tok2)
                r.op("vector", lambda e, h=h, bankU=bankU: e.tensor_tensor(
                    out=b_U[h][:, :], in0=bankU[:, 0:256], in1=b_U[h][:, :],
                    op=OP.add), sig=V)
                upe_tok = V.n
                brel(2 + p, V)
                r.wait("tensor", V, upe_tok)
                for hf in range(2):
                    mm(bankA[:, 128:256], khfv[:, 2 * h + hf, ts],
                       qfv[:, 2 * h + hf, ts], hf == 0, hf == 1,
                       sig=T if hf == 1 else None)
                gt_tok = T.n
                r.wait("vector", T, gt_tok)
                r.op("vector", lambda e, h=h, bankA=bankA: e.tensor_tensor(
                    out=b_GT[h][:, :], in0=bankA[:, 128:256], in1=c_MT[:],
                    op=OP.mult), sig=V)
                gte_tok = V.n
                r.wait("tensor", V, gte_tok)
                if not first:
                    for hf in range(2):
                        mm(bankA[:, 256:512], qfv[:, 2 * h + hf, ts],
                           b_S[h][:, 256 * hf:256 * (hf + 1)], hf == 0, False)
                    mm(bankA[:, 256:512], b_GT[h][:, :], b_U[h][:, :], False,
                       True, sig=T)
                else:
                    mm(bankA[:, 256:512], b_GT[h][:, :], b_U[h][:, :], True,
                       True, sig=T)
                o_tok = T.n
                for hf in range(2):
                    mm(ps[4 + h][:, 256 * hf:256 * (hf + 1)],
                       b_ktok[:, 1024 * blk + 256 * h + 128 * hf:
                               1024 * blk + 256 * h + 128 * (hf + 1)],
                       b_U[h][:, :], first and hf == 0, False,
                       sig=T if hf == 1 else None, sgc=True)
                sup_tok[h] = T.n
                r.wait("vector", T, o_tok)
                r.op("vector", lambda e, blk=blk, h=h: e.tensor_scalar(
                    out=b_jnkS[:, 0:256],
                    in0=b_vtok[:, 1024 * blk + 256 * h:1024 * blk + 256 * (h + 1)],
                    scalar1=b_bx[:, 8 * blk + 4 + h:8 * blk + 5 + h],
                    scalar2=None, op0=OP.mult))
                r.op("vector", lambda e, blk=blk, h=h, bankA=bankA:
                     e.scalar_tensor_tensor(
                         out=b_o2[:, 256 * h:256 * (h + 1)],
                         in0=bankA[:, 256:512],
                         scalar=b_rn[:, 16 * blk + 8 + h:16 * blk + 9 + h],
                         in1=b_jnkS[:, 0:256], op0=OP.mult, op1=OP.add), sig=V)
                o2_tok = V.n
                brel(p, V)
                r.wait("scalar", V, o2_tok)
                r.op("scalar", lambda e, h=h: e.activation(
                    out=b_jnkS[:, 256:512], in_=b_o2[:, 256 * h:256 * (h + 1)],
                    func=AF.Square, accum_out=b_acc[:, 32 + h:33 + h]),
                    sig=S if h == 3 else None)
            r.wait("vector", S)
            rsqrt(b_rn[:, 16 * blk + 12:16 * blk + 16], b_acc[:, 32:36],
                  1e-5, 4, scale=1.0 / 256.0)
            for h in range(H):
                r.op("vector", lambda e, blk=blk, h=h: e.tensor_scalar(
                    out=b_onrm[:, 1024 * blk + 256 * h:1024 * blk + 256 * (h + 1)],
                    in0=b_o2[:, 256 * h:256 * (h + 1)],
                    scalar1=b_rn[:, 16 * blk + 12 + h:16 * blk + 13 + h],
                    scalar2=None, op0=OP.mult), sig=V if h == 3 else None)
            onrm_tok[blk] = V.n

        for tt in range(NTT):
            gidx = seg * NTT + tt
            r.wait("tensor", V, onrm_tok[tt])
            bwait(2)
            for ch in range(8):
                tr(pb(2)[0:128, 128 * ch:128 * (ch + 1)],
                   b_onrm[:, 1024 * tt + 128 * ch:1024 * tt + 128 * (ch + 1)],
                   c_idB[:], sig=T if ch == 7 else None)
            r.wait("scalar", T)
            r.op("scalar", lambda e: e.copy(
                out=b_oT, in_=pb(2)[:, 0:1024]), sig=S)
            brel(2, S)
            r.wait("tensor", S)
            bwait(0)
            bwait(1)
            for nh in range(2):
                for ch in range(8):
                    mm(ps[nh][:, 0:512], b_oT[:, 128 * ch:128 * (ch + 1)],
                       w_oT[:].rearrange("p (i o) -> p i o", o=D)
                       [:, ch, 512 * nh:512 * (nh + 1)],
                       ch == 0, ch == 7, sig=T if (nh == 1 and ch == 7) else None)
            r.wait("vector", T)
            for nh in range(2):
                r.op("vector", lambda e, tt=tt, nh=nh: e.tensor_tensor(
                    out=b_x1[:, 512 * nh:512 * (nh + 1)], in0=ps[nh][:, 0:512],
                    in1=b_x0t[:, 1024 * tt + 512 * nh:1024 * tt + 512 * (nh + 1)],
                    op=OP.add), sig=V if nh == 1 else None)
            x1ev = V.n
            brel(0, V)
            brel(1, V)
            if has_next:
                A_load(seg + 1, tt, xgate=x1ev)
            r.wait("scalar", V)
            r.op("scalar", lambda e: e.activation(
                out=b_jnkS, in_=b_x1[:], func=AF.Copy,
                accum_out=b_acc[:, 40:41]))
            r.op("scalar", lambda e: e.activation(
                out=b_jnkS, in_=b_x1[:], func=AF.Square,
                accum_out=b_acc[:, 41:42]), sig=S)
            ln2_tok = S.n
            if has_next:
                A_stats(seg + 1, tt)
                if tt == 1:
                    A_apply(seg + 1, tt)
            r.wait("vector", S, ln2_tok)
            m2 = b_ln2[:, (2 * gidx) % 32:(2 * gidx) % 32 + 1]
            r2 = b_ln2[:, (2 * gidx) % 32 + 1:(2 * gidx) % 32 + 2]
            r.op("vector", lambda e, m2=m2: e.tensor_scalar(
                out=m2, in0=b_acc[:, 40:41], scalar1=1.0 / D, scalar2=None,
                op0=OP.mult))
            r.op("vector", lambda e, m2=m2: e.tensor_tensor(
                out=b_jnkV[:, 16:17], in0=m2, in1=m2, op=OP.mult))
            r.op("vector", lambda e: e.scalar_tensor_tensor(
                out=b_acc[:, 42:43], in0=b_acc[:, 41:42], scalar=1.0 / D,
                in1=b_jnkV[:, 16:17], op0=OP.mult, op1=OP.subtract))
            rsqrt(r2, b_acc[:, 42:43], 1e-5, 1)
            r.op("vector", lambda e, m2=m2, r2=r2: e.tensor_scalar(
                out=b_xr[:], in0=b_x1[:], scalar1=m2, scalar2=r2,
                op0=OP.subtract, op1=OP.mult))
            r.op("vector", lambda e, gidx=gidx: e.tensor_reduce(
                out=b_mxp[:, gidx % 16:gidx % 16 + 1], in_=b_xr[:], axis=AX,
                op=OP.max, apply_absolute_value=True), sig=V)
            if gidx == 0:
                r.op("vector", lambda e: e.tensor_copy(
                    out=b_mxp[:, 16:17], in_=b_mxp[:, 0:1]), sig=V)
            else:
                r.op("vector", lambda e, gidx=gidx: e.tensor_tensor(
                    out=b_mxp[:, 16:17], in0=b_mxp[:, 16:17],
                    in1=b_mxp[:, gidx % 16:gidx % 16 + 1], op=OP.max), sig=V)
            r.wait("sync", V)
            r.op("sync", lambda e, tt=tt: e.dma_start(
                out=d_x1sp[t0 + 128 * tt:t0 + 128 * (tt + 1), :], in_=b_x1[:]),
                sig=P_X1ST, sign=16)
            r.wait("vector", P_X1ST)
            if has_next and tt == 0:
                A_apply(seg + 1, tt)
        if has_next:
            A_transpose(seg + 1, 0, 3)
            A_transpose(seg + 1, 1, 2)

    def fence():
        for eng in _ENGS:
            for sem in (T, V, S, P_CST, P_SHARD, P_X00, P_X01, P_X1ST,
                        P_X1LD, P_UPSP, P_UPLD, P_WST[0], P_WST[1], P_DN,
                        P_BAR, P_OUT, P_WGT, P_AGWR, P_GLD, P_GLD2):
                r.wait(eng, sem)

    def barrier_max2(slot, col_ap, n=1):
        mi, mo = (d_mxi, d_mxo) if n == 1 else (d_mxi2, d_mxo2)
        r.wait("sync", V)
        r.wait("sync", P_BAR)
        if n == 1:
            r.op("sync", lambda e: e.dma_start(
                out=d_mxi, in_=col_ap), sig=P_BAR, sign=16)
        else:
            for s in range(n):
                r.op("sync", lambda e, s=s: e.dma_start(
                    out=d_mxi2[s:s + 1, :], in_=col_ap[:, s:s + 1]),
                    sig=P_BAR, sign=16)
        r.wait("gpsimd", P_BAR)
        r.op("gpsimd", lambda e, mi=mi, mo=mo: e.collective_compute(
            "AllReduce", OP.max, replica_groups=[list(range(8))],
            ins=[mi.tensor.ap().opt()], outs=[mo.tensor.ap().opt()]),
            sig=CC)
        r.wait("sync", CC)
        if n == 1:
            r.op("sync", lambda e: e.dma_start(
                out=b_row[0:1, 0:128],
                in_=d_mxo.rearrange("p one -> one p")), sig=P_BAR, sign=16)
        else:
            r.op("sync", lambda e: e.dma_start(
                out=b_row[0:2, 0:128], in_=d_mxo2), sig=P_BAR, sign=16)
        r.wait("vector", P_BAR)
        r.op("vector", lambda e, n=n: e.tensor_reduce(
            out=b_sc1[0:n, 0:1], in_=b_row[0:n, 0:128], axis=AX, op=OP.max),
            sig=V)
        r.wait("sync", V)
        r.op("sync", lambda e, slot=slot, n=n: e.dma_start(
            out=d_sc[0:1, slot:slot + n], in_=b_sc1[0:n, 0:1]), sig=P_BAR, sign=16)
        r.wait("sync", P_BAR)
        r.op("sync", lambda e, slot=slot, n=n: e.dma_start(
            out=b_rs[:, slot:slot + n],
            in_=d_sc[0:1, slot:slot + n].broadcast_to((128, n))), sig=P_BAR, sign=16)
        r.wait("vector", P_BAR)
        r.wait("scalar", P_BAR)

    def ph2():
        fence()
        r.wait("sync", CC)
        r.wait("scalar", CC)
        for oc in range(2):
            sc_, lo = oc // 3, (oc % 3) * 128
            for wi, woff in ((0, 0), (1, D * OSH)):
                r.op("scalar", lambda e, sc_=sc_, lo=lo, woff=woff, wi=wi,
                     oc=oc: e.dma_start(
                         out=_wst(oc % 2, wi).rearrange("p (c x) -> p c x", x=128),
                         in_=d_ago[sc_, woff:woff + D * OSH]
                         .rearrange("(i o) -> i o", o=OSH)[:, lo:lo + 128]
                         .rearrange("(c p) o -> p c o", p=128)),
                     sig=P_WST[oc % 2], sign=16, dma=True)
        x1t = [b_x1[:], b_jnkS[:]]
        xlnb = [b_xln[:], b_oT[:]]
        ld2_tok = [0] * NTILE
        cons_tok = [0] * NTILE
        tr_tok = [0] * NTILE
        cp_tok = [0] * NTILE

        def Ld2(t):
            pool = (P_X1LD, P_X1ST)[t % 2]
            if t >= 2:
                r.wait("scalar", V, cons_tok[t - 2])
            r.op("scalar", lambda e, t=t: e.dma_start(
                out=x1t[t % 2], in_=d_x1sp[128 * t:128 * (t + 1), :]),
                sig=pool, sign=16, dma=True)
            ld2_tok[t] = pool.n

        Ld2(0)
        Ld2(1)
        barrier_max2(0, b_mxp[:, 16:17])
        r.op("vector", lambda e: e.tensor_scalar(
            out=b_rs[:, 1:2], in0=b_rs[:, 0:1], scalar1=1.0 / 32767.0,
            scalar2=1e-12, op0=OP.mult, op1=OP.max))
        r.op("vector", lambda e: e.reciprocal(
            out=b_rs[:, 2:3], in_=b_rs[:, 1:2]), sig=V)
        for tile in range(NTILE):
            m2 = b_ln2[:, (2 * tile) % 32:(2 * tile) % 32 + 1]
            r2 = b_ln2[:, (2 * tile) % 32 + 1:(2 * tile) % 32 + 2]
            r.wait("vector", (P_X1LD, P_X1ST)[tile % 2], ld2_tok[tile])
            r.op("vector", lambda e, r2=r2: e.tensor_tensor(
                out=b_jnkV[:, 0:1], in0=r2, in1=b_rs[:, 2:3], op=OP.mult))
            r.op("vector", lambda e, m2=m2, tile=tile: e.tensor_scalar(
                out=b_xr[:], in0=x1t[tile % 2], scalar1=m2,
                scalar2=b_jnkV[:, 0:1], op0=OP.subtract, op1=OP.mult))
            r.op("vector", lambda e: e.tensor_scalar(
                out=b_o2.bitcast(I16)[:, 0:1024], in0=b_xr[:],
                scalar1=32767.0, scalar2=None, op0=OP.min))
            if tile >= 2:
                r.wait("vector", T, tr_tok[tile - 2])
            r.op("vector", lambda e, tile=tile: e.tensor_copy(
                out=xlnb[tile % 2], in_=b_o2.bitcast(I16)[:, 0:1024]), sig=V)
            cons_tok[tile] = V.n
            if tile + 2 < NTILE:
                Ld2(tile + 2)
            bk = 2 + tile % 2
            r.wait("tensor", V, cons_tok[tile])
            if tile >= 2:
                r.wait("tensor", S, cp_tok[tile - 2])
            for ch in range(8):
                tr(pb(bk)[0:128, 128 * ch:128 * (ch + 1)],
                   xlnb[tile % 2][:, 128 * ch:128 * (ch + 1)], c_idB[:],
                   sig=T if ch == 7 else None)
            tr_tok[tile] = T.n
            r.wait("scalar", T, tr_tok[tile])
            r.op("scalar", lambda e, tile=tile, bk=bk: e.copy(
                out=xqT_A[:, 0:10240].rearrange("p (c x) -> p c x", x=2048)
                [:, :, 128 * tile:128 * (tile + 1)],
                in_=pb(bk)[:, 0:640].rearrange("p (c x) -> p c x", x=128)))
            r.op("scalar", lambda e, tile=tile, bk=bk: e.copy(
                out=xqT_B[:, 0:6144].rearrange("p (c x) -> p c x", x=2048)
                [:, :, 128 * tile:128 * (tile + 1)],
                in_=pb(bk)[:, 640:1024].rearrange("p (c x) -> p c x", x=128)),
                sig=S)
            cp_tok[tile] = S.n

    def _wst(dbuf, wi):
        return b_wst[:, 2048 * dbuf + 1024 * wi:2048 * dbuf + 1024 * (wi + 1)]

    def ph3():
        n_oc = int(os.environ.get("BASSMK_F1OC", str(NOC)))
        r.wait("gpsimd", CC)
        r.wait("gpsimd", T)
        for oc in range(NOC):
            for c8 in range(8):
                r.op("gpsimd", lambda e, oc=oc, c8=c8: e.dma_start(
                    out=w_dnT[:, D * oc + 128 * c8:D * oc + 128 * (c8 + 1)],
                    in_=d_ago[c8, 2 * D * OSH + 128 * oc * 128:
                              2 * D * OSH + 128 * (oc + 1) * 128]
                    .rearrange("(p o) -> p o", o=128)), sig=P_DN, sign=16)
        dn_tok = P_DN.n
        r.wait("vector", S)
        r.op("vector", lambda e: e.memset(b_mxp[:, 17:19], 0.0), sig=V)
        last_mm_tok = [0, 0]
        wst_tok = [P_WST[0].n, P_WST[1].n]
        sp_tok = [0] * 4
        s0 = S.n
        gen_free = [[S, s0]] * 8
        gen = 0
        for oc in range(n_oc):
            dbuf = oc % 2
            r.wait("tensor", P_WST[dbuf], wst_tok[dbuf])
            for half in range(2):
                pbase = 4 * (gen % 2)
                for b4 in range(4):
                    sem, tok = gen_free[pbase + b4]
                    r.wait("tensor", sem, tok)
                for wi in range(2):
                    for th in range(2):
                        bank = ps[pbase + 2 * wi + th]
                        for it in range(8):
                            mm(bank[:, 0:512],
                               _wst(dbuf, wi)[:, 128 * it:128 * (it + 1)],
                               xq_sl(it, 1024 * half + 512 * th, 512),
                               it == 0, it == 7,
                               sig=T if (wi == 1 and th == 1 and it == 7)
                               else None)
                mm_tok = T.n
                last_mm_tok[dbuf] = mm_tok
                r.wait("scalar", T, mm_tok)
                spool = (P_UPSP, P_UPLD)[half]
                r.wait("scalar", spool, sp_tok[half])
                for wi in range(2):
                    for th in range(2):
                        bank = ps[pbase + 2 * wi + th]
                        if wi == 0:
                            r.op("scalar", lambda e, th=th, half=half,
                                 bank=bank: e.copy(
                                     out=v_fst[:, 1024 * half + 512 * th:
                                               1024 * half + 512 * (th + 1)],
                                     in_=bank[:, 0:512]), sig=S)
                        else:
                            r.op("scalar", lambda e, oc=oc, th=th, half=half,
                                 bank=bank: e.copy(
                                     out=g_slot(oc)
                                     [:, 1024 * half + 512 * th:
                                      1024 * half + 512 * (th + 1)],
                                     in_=bank[:, 0:512]), sig=S)
                cp_tok = S.n
                r.wait("sync", S, cp_tok)
                r.op("sync", lambda e, oc=oc, half=half: e.dma_start(
                    out=d_upsp[oc][:, 1024 * half:1024 * (half + 1)],
                    in_=v_fst[:, 1024 * half:1024 * (half + 1)]),
                    sig=spool, sign=16)
                sp_tok[half] = spool.n
                r.wait("vector", S, cp_tok)
                for wi in range(2):
                    for th in range(2):
                        bank = ps[pbase + 2 * wi + th]
                        r.op("vector", lambda e, bank=bank, wi=wi, th=th:
                             e.tensor_reduce(
                                 out=b_jnkV[:, 1 + 4 * wi + th:2 + 4 * wi + th]
                                 if False else
                                 b_jnkV[:, 1 + 2 * wi + th:2 + 2 * wi + th],
                                 in_=bank[:, 0:512], axis=AX, op=OP.max,
                                 apply_absolute_value=True), sig=V)
                mx_tok = V.n
                for b4 in range(4):
                    gen_free[pbase + b4] = [V, mx_tok]
                r.op("vector", lambda e: e.tensor_reduce(
                    out=b_jnkV[:, 9:10], in_=b_jnkV[:, 1:3], axis=AX, op=OP.max))
                r.op("vector", lambda e: e.tensor_tensor(
                    out=b_mxp[:, 17:18], in0=b_mxp[:, 17:18],
                    in1=b_jnkV[:, 9:10], op=OP.max))
                r.op("vector", lambda e: e.tensor_reduce(
                    out=b_jnkV[:, 9:10], in_=b_jnkV[:, 3:5], axis=AX, op=OP.max))
                r.op("vector", lambda e: e.tensor_tensor(
                    out=b_mxp[:, 18:19], in0=b_mxp[:, 18:19],
                    in1=b_jnkV[:, 9:10], op=OP.max), sig=V)
                gen += 1
            if oc + 2 < n_oc:
                oc2 = oc + 2
                sc_, lo = oc2 // 3, (oc2 % 3) * 128
                r.wait("sync", T, last_mm_tok[dbuf])
                for wi, woff in ((0, 0), (1, D * OSH)):
                    r.op("sync", lambda e, sc_=sc_, lo=lo, woff=woff, wi=wi,
                         dbuf=dbuf: e.dma_start(
                             out=_wst(dbuf, wi)
                             .rearrange("p (c x) -> p c x", x=128),
                             in_=d_ago[sc_, woff:woff + D * OSH]
                             .rearrange("(i o) -> i o", o=OSH)[:, lo:lo + 128]
                             .rearrange("(c p) o -> p c o", p=128)),
                         sig=P_WST[dbuf], sign=16)
                wst_tok[dbuf] = P_WST[dbuf].n
        barrier_max2(3, b_mxp[:, 17:19], n=2)
        for gslot, dst in ((3, 5), (4, 7)):
            r.op("vector", lambda e, gslot=gslot: e.tensor_tensor(
                out=b_jnkV[:, 0:1], in0=b_rs[:, gslot:gslot + 1],
                in1=b_rs[:, 1:2], op=OP.mult))
            r.op("vector", lambda e, dst=dst: e.tensor_scalar(
                out=b_rs[:, dst:dst + 1], in0=b_jnkV[:, 0:1],
                scalar1=1.0 / 32767.0, scalar2=1e-12, op0=OP.mult, op1=OP.max))
            r.op("vector", lambda e, dst=dst: e.reciprocal(
                out=b_jnkV[:, 1:2], in_=b_rs[:, dst:dst + 1]))
            r.op("vector", lambda e, dst=dst: e.tensor_tensor(
                out=b_rs[:, dst + 1:dst + 2], in0=b_rs[:, 1:2],
                in1=b_jnkV[:, 1:2], op=OP.mult), sig=V)
        return dn_tok

    def ph4():
        r.wait("scalar", T)
        r.wait("vector", T)
        r.wait("vector", S)
        r.wait("gpsimd", T)
        r.wait("gpsimd", P_UPSP)
        r.wait("gpsimd", P_UPLD)
        r.wait("vector", P_UPSP)
        r.wait("vector", P_UPLD)
        r.wait("sync", P_X1LD)
        r.wait("sync", P_X1ST)
        r.wait("vector", P_X1LD)
        r.wait("vector", P_X1ST)
        tokVg = [0] * NOC
        tokSs = [0] * NOC
        tokVu = [0] * NOC
        tokGt = [0] * NOC
        tokVm = [0] * NOC

        def Vg(i):
            r.op("vector", lambda e, i=i: e.tensor_scalar(
                out=g_slot(i).bitcast(I16), in0=g_slot(i),
                scalar1=b_rs[:, 8:9], scalar2=32767.0,
                op0=OP.mult, op1=OP.min), sig=V)
            tokVg[i] = V.n

        sg_tok = [0] * NOC
        tokCv = [0] * NOC
        tokT2 = [0] * NOC
        sigTmp = AR_A[:, 8192:12288].bitcast(F32)

        def Ss(i):
            p = i % 2
            r.wait("scalar", V, tokVg[i])
            if i >= 2:
                r.wait("scalar", V, tokVu[i - 2])
            r.op("scalar", lambda e, i=i, p=p: e.activation(
                out=t_sigF[p], in_=g_slot(i).bitcast(I16),
                func=AF.Sigmoid, scale=b_rs[:, 7:8]), sig=S)
            tokSs[i] = S.n

        Vg(0)
        Vg(1)
        Ss(0)
        for i in range(NOC):
            p = i % 2
            r.wait("vector", S, tokSs[i])
            if i >= 1:
                r.wait("vector", S, tokCv[i - 1])
            r.op("vector", lambda e, p=p: e.tensor_scalar(
                out=sigTmp, in0=t_sigF[p], scalar1=65536.0,
                scalar2=65535.0, op0=OP.mult, op1=OP.min), sig=V)
            tokVu[i] = V.n
            if i + 1 < NOC:
                Ss(i + 1)
            r.wait("scalar", V, tokVu[i])
            if i >= 2:
                r.wait("scalar", V, tokT2[i - 2])
                r.wait("scalar", (P_X1LD, P_X1ST)[p], sg_tok[i - 2])
            r.op("scalar", lambda e, p=p: e.copy(
                out=t_sigU[p], in_=sigTmp), sig=S)
            tokCv[i] = S.n
            spool = (P_X1LD, P_X1ST)[p]
            r.wait("sync", S, tokCv[i])
            r.op("sync", lambda e, i=i, p=p: e.dma_start(
                out=d_sgu[i], in_=t_sigU[p]), sig=spool, sign=16)
            sg_tok[i] = spool.n
            if i + 2 < NOC:
                Vg(i + 2)
            if i >= 1:
                r.op("vector", lambda e, i=i: e.tensor_reduce(
                    out=b_jnkV[:, i - 1:i], in_=t_prod[(i - 1) % 2], axis=AX,
                    op=OP.max, apply_absolute_value=True), sig=V)
                tokVm[i - 1] = V.n
            r.wait("vector", S, tokCv[i])
            r.op("vector", lambda e, i=i, p=p: e.tensor_tensor(
                out=t_prod[p], in0=g_slot(i).bitcast(I16), in1=t_sigU[p],
                op=OP.mult), sig=V)
            tokT2[i] = V.n
        r.op("vector", lambda e: e.tensor_reduce(
            out=b_jnkV[:, NOC - 1:NOC], in_=t_prod[(NOC - 1) % 2], axis=AX,
            op=OP.max, apply_absolute_value=True))
        r.op("vector", lambda e: e.tensor_reduce(
            out=b_mxp[:, 17:18], in_=b_jnkV[:, 0:NOC], axis=AX, op=OP.max),
            sig=V)
        barrier_max2(13, b_mxp[:, 17:18])
        r.op("vector", lambda e: e.tensor_scalar(
            out=b_jnkV[:, 0:1], in0=b_rs[:, 7:8], scalar1=1.0 / 65536.0,
            scalar2=None, op0=OP.mult))
        r.op("vector", lambda e: e.tensor_tensor(
            out=b_jnkV[:, 1:2], in0=b_jnkV[:, 0:1], in1=b_rs[:, 13:14],
            op=OP.mult))
        r.op("vector", lambda e: e.tensor_scalar(
            out=b_rs[:, 9:10], in0=b_jnkV[:, 1:2], scalar1=1.0 / 32767.0,
            scalar2=1e-12, op0=OP.mult, op1=OP.max))
        r.op("vector", lambda e: e.reciprocal(
            out=b_jnkV[:, 2:3], in_=b_rs[:, 9:10]))
        r.op("vector", lambda e: e.tensor_tensor(
            out=b_rs[:, 10:11], in0=b_jnkV[:, 0:1], in1=b_jnkV[:, 2:3],
            op=OP.mult), sig=V)

    def ph5():
        r.wait("scalar", V)
        r.wait("gpsimd", V)
        r.wait("sync", T)
        r.wait("sync", P_UPSP)
        r.wait("vector", P_UPSP)
        r.wait("vector", P_UPLD)
        r.wait("gpsimd", P_UPLD)
        r.wait("scalar", P_X1LD)
        r.wait("scalar", P_X1ST)
        r.wait("gpsimd", P_X1LD)
        r.wait("gpsimd", P_X1ST)
        tokGt2 = [0] * NOC
        tokVg2 = [0] * NOC
        tokVc = [0] * NOC
        tokGt3 = [(V, 0)] * NOC
        tokVm = [0] * NOC
        ld_tok = [0] * NOC
        prod4 = [t_prod[0], t_prod[1], t_sigF[0], t_sigF[1]]

        def Ld(i):
            pool = (P_UPSP, P_GLD)[i % 2]
            eng = ("sync", "gpsimd")[i % 2]
            if i >= 2:
                sem, tok = tokGt3[i - 2]
                r.wait(eng, sem, tok)
            r.op(eng, lambda e, i=i: e.dma_start(
                out=t_upb[i % 2], in_=d_upsp[i]), sig=pool, sign=16)
            ld_tok[i] = pool.n

        cv2 = [0] * NOC

        def Vg2(i):
            r.op("vector", lambda e, i=i: e.tensor_scalar(
                out=prod4[i % 4], in0=prod4[i % 4],
                scalar1=b_rs[:, 10:11], scalar2=32767.0,
                op0=OP.mult, op1=OP.min), sig=V)
            tokVg2[i] = V.n
            r.wait("scalar", V, tokVg2[i])
            r.op("scalar", lambda e, i=i: e.copy(
                out=g_slot(i).bitcast(I16), in_=prod4[i % 4]), sig=S)
            cv2[i] = S.n

        def Gt3(i):
            r.wait("vector", S, cv2[i])
            r.op("vector", lambda e, i=i: e.tensor_tensor(
                out=prod4[i % 4], in0=g_slot(i).bitcast(I16),
                in1=t_upb[i % 2].bitcast(I16), op=OP.mult), sig=V)
            tokGt3[i] = (V, V.n)

        def Vm(i):
            r.op("vector", lambda e, i=i: e.tensor_reduce(
                out=b_jnkV[:, i:i + 1], in_=prod4[i % 4], axis=AX,
                op=OP.max, apply_absolute_value=True), sig=V)
            tokVm[i] = V.n

        sgld_tok = [0] * NOC

        def LdSg(i):
            spool = (P_X1LD, P_X1ST)[i % 2]
            if i >= 2:
                r.wait("scalar", V, tokGt2[i - 2])
            r.op("scalar", lambda e, i=i: e.dma_start(
                out=t_sigU[i % 2], in_=d_sgu[i]), sig=spool, sign=16,
                dma=True)
            sgld_tok[i] = spool.n

        Ld(0)
        Ld(1)
        LdSg(0)
        LdSg(1)
        for i in range(NOC):
            p = i % 2
            if i >= 2:
                Vm(i - 2)
            if i >= 1 and i + 1 < NOC:
                LdSg(i + 1)
            pool = (P_UPSP, P_GLD)[p]
            r.wait("vector", pool, ld_tok[i])
            r.op("vector", lambda e, p=p: e.tensor_scalar(
                out=t_upb[p].bitcast(I16), in0=t_upb[p],
                scalar1=b_rs[:, 6:7], scalar2=32767.0,
                op0=OP.mult, op1=OP.min), sig=V)
            tokVc[i] = V.n
            r.wait("vector", (P_X1LD, P_X1ST)[p], sgld_tok[i])
            r.op("vector", lambda e, i=i, p=p: e.tensor_tensor(
                out=prod4[i % 4], in0=g_slot(i).bitcast(I16), in1=t_sigU[p],
                op=OP.mult), sig=V)
            tokGt2[i] = V.n
            Vg2(i)
            if i >= 1:
                Gt3(i - 1)
                if i + 1 < NOC:
                    Ld(i + 1)
        Gt3(NOC - 1)
        Vm(NOC - 2)
        Vm(NOC - 1)
        r.op("vector", lambda e: e.tensor_reduce(
            out=b_mxp[:, 18:19], in_=b_jnkV[:, 0:NOC], axis=AX, op=OP.max),
            sig=V)
        barrier_max2(14, b_mxp[:, 18:19])
        r.op("vector", lambda e: e.tensor_tensor(
            out=b_jnkV[:, 0:1], in0=b_rs[:, 9:10], in1=b_rs[:, 5:6],
            op=OP.mult))
        r.op("vector", lambda e: e.tensor_tensor(
            out=b_jnkV[:, 1:2], in0=b_jnkV[:, 0:1], in1=b_rs[:, 14:15],
            op=OP.mult))
        r.op("vector", lambda e: e.tensor_scalar(
            out=b_rs[:, 11:12], in0=b_jnkV[:, 1:2], scalar1=1.0 / 32767.0,
            scalar2=1e-12, op0=OP.mult, op1=OP.max))
        r.op("vector", lambda e: e.reciprocal(
            out=b_jnkV[:, 2:3], in_=b_rs[:, 11:12]))
        r.op("vector", lambda e: e.tensor_tensor(
            out=b_rs[:, 12:13], in0=b_jnkV[:, 0:1], in1=b_jnkV[:, 2:3],
            op=OP.mult), sig=V)

    def ph6(dn_tok):
        r.wait("scalar", V)
        r.wait("scalar", G)
        r.wait("gpsimd", V)
        r.wait("sync", V)
        r.wait("vector", P_X1LD)
        r.wait("vector", P_X1ST)
        r.wait("vector", P_OUT)
        r.wait("vector", P_X00)
        r.wait("vector", P_UPSP)
        r.wait("vector", P_UPLD)
        r.wait("tensor", P_DN, dn_tok)
        nj = 4 * NOC
        tokLd = [0] * nj
        tokVc = [0] * nj
        tokG = [0] * nj
        tokVo = [0] * nj
        tokSc = [0] * nj
        scq_tok = [0] * 4
        x1_tok = [0] * NTILE
        ev_tok = [0] * NTILE
        out_tok = [0] * NTILE
        x1_emitted = [False] * NTILE
        bank_tok = {b: [V, V.n] for b in range(4)}

        ld_pools = (P_UPSP, P_GLD, P_UPLD, P_GLD2)
        ld_engs = ("sync", "gpsimd", "sync", "gpsimd")

        def Ld6(j):
            pool = ld_pools[j % 4]
            eng = ld_engs[j % 4]
            q, i = divmod(j, NOC)
            if j >= 4:
                r.wait(eng, G, tokG[j - 4])
            r.op(eng, lambda e, q=q, i=i, j=j: e.dma_start(
                out=t_u4[j % 4], in_=d_upsp[i][:, 512 * q:512 * (q + 1)]),
                sig=pool, sign=16)
            tokLd[j] = pool.n

        def LdX1(t):
            pool = (P_X1LD, P_X1ST)[t % 2]
            if t >= 2:
                r.wait("scalar", V, ev_tok[t - 2])
            r.op("scalar", lambda e, t=t: e.dma_start(
                out=x1b[t % 2], in_=d_x1sp[128 * t:128 * (t + 1), :]),
                sig=pool, sign=16, dma=True)
            x1_tok[t] = pool.n
            x1_emitted[t] = True

        def emit_ew(q, i):
            j = q * NOC + i
            b4 = j % 4
            if j + 3 < nj:
                Ld6(j + 3)
            pool = ld_pools[j % 4]
            r.wait("vector", pool, tokLd[j])
            r.op("vector", lambda e, b4=b4: e.tensor_scalar(
                out=t_u4[b4].bitcast(I16), in0=t_u4[b4],
                scalar1=b_rs[:, 6:7], scalar2=32767.0,
                op0=OP.mult, op1=OP.min), sig=V)
            tokVc[j] = V.n
            r.wait("gpsimd", V, tokVc[j])
            if j >= 4:
                r.wait("gpsimd", V, tokVo[j - 4])
            r.op("gpsimd", lambda e, q=q, i=i, b4=b4: e.tensor_tensor(
                out=t_p4[b4],
                in0=g_slot(i).bitcast(I16)[:, 512 * q:512 * (q + 1)],
                in1=t_u4[b4].bitcast(I16), op=OP.mult), sig=G)
            tokG[j] = G.n
            r.wait("vector", G, tokG[j])
            if j >= 4:
                r.wait("vector", S, tokSc[j - 4])
            r.op("vector", lambda e, b4=b4: e.tensor_scalar(
                out=t_oI[:, 512 * b4:512 * (b4 + 1)], in0=t_p4[b4],
                scalar1=b_rs[:, 12:13], scalar2=32767.0,
                op0=OP.mult, op1=OP.min), sig=V)
            tokVo[j] = V.n
            r.wait("scalar", V, tokVo[j])
            r.op("scalar", lambda e, q=q, i=i, b4=b4: e.copy(
                out=g_slot(i)[:, 512 * q:512 * (q + 1)],
                in_=t_oI[:, 512 * b4:512 * (b4 + 1)]), sig=S)
            tokSc[j] = S.n

        mm_tok = [0] * NTILE

        def emit_mm(t):
            r.wait("tensor", S, scq_tok[t // 4])
            for o2h in range(2):
                b = (2 * t + o2h) % 4
                sem, tok = bank_tok[b]
                r.wait("tensor", sem, tok)
                for oc in range(NOC):
                    mm(ps[b][:, 0:512],
                       g_slot(oc)[:, 128 * t:128 * (t + 1)],
                       w_dnT[:, D * oc + 512 * o2h:D * oc + 512 * (o2h + 1)],
                       oc == 0, oc == NOC - 1,
                       sig=T if oc == NOC - 1 else None)
            mm_tok[t] = T.n

        def emit_fin(t):
            r.wait("vector", T, mm_tok[t])
            xpool = (P_X1LD, P_X1ST)[t % 2]
            r.wait("vector", xpool, x1_tok[t])
            opool = (P_OUT, P_X00)[t % 2]
            if t >= 2:
                r.wait("vector", opool, out_tok[t - 2])
            for o2h in range(2):
                b = (2 * t + o2h) % 4
                r.op("vector", lambda e, b=b, o2h=o2h, t=t:
                     e.scalar_tensor_tensor(
                         out=xrb[t % 2][:, 512 * o2h:512 * (o2h + 1)],
                         in0=ps[b][:, 0:512], scalar=b_rs[:, 11:12],
                         in1=x1b[t % 2][:, 512 * o2h:512 * (o2h + 1)],
                         op0=OP.mult, op1=OP.add), sig=V if o2h == 1 else None)
            ev_tok[t] = V.n
            for o2h in range(2):
                bank_tok[(2 * t + o2h) % 4] = [V, ev_tok[t]]
            r.wait("sync", V, ev_tok[t])
            r.op("sync", lambda e, t=t: e.dma_start(
                out=d_out[128 * t:128 * (t + 1), :], in_=xrb[t % 2]),
                sig=opool, sign=16)
            out_tok[t] = opool.n
            if t + 2 < NTILE:
                LdX1(t + 2)

        for j in range(3):
            Ld6(j)
        LdX1(0)
        LdX1(1)
        pend = []
        for i in range(NOC):
            emit_ew(0, i)
        scq_tok[0] = S.n
        for q in range(1, 5):
            for i in range(NOC):
                if q < 4:
                    emit_ew(q, i)
                if i % 6 == 5:
                    t = 4 * (q - 1) + i // 6
                    emit_mm(t)
                    if pend:
                        emit_fin(pend.pop(0))
                    pend.append(t)
            if q < 4:
                scq_tok[q] = S.n
        for t in pend:
            emit_fin(t)
        r.wait("sync", P_OUT)
        r.wait("sync", P_X00)
        r.wait("tensor", P_OUT)
        r.wait("tensor", P_X00)

    phmax = int(os.environ.get("BASSMK_PHASES", "9"))
    nseg = int(os.environ.get("BASSMK_SEGS", str(NSEG)))
    with es:
        block = es.enter_context(nc.Block())
        ph0()
        if phmax >= 1:
            for seg in range(nseg):
                if seg == 0:
                    fence()
                attn_segment(seg, seg + 1 < nseg)
        if phmax >= 2 and nseg == NSEG:
            ph2()
        if phmax >= 3 and nseg == NSEG:
            dn_tok = ph3()
            if phmax >= 4:
                ph4()
            if phmax >= 5:
                ph5()
            if phmax >= 6:
                ph6(dn_tok)
        fence()
        r.flush(block)
    return nc


_NC_CACHE = None


def kernel(**inputs):
    global _NC_CACHE
    in_maps = _host_prepare(inputs)
    if _NC_CACHE is None:
        _NC_CACHE = _build()
    res = run_bass_kernel_spmd(_NC_CACHE, in_maps, core_ids=list(range(8)))
    out = np.stack([rr["out"] for rr in res.results], 0)
    return out.reshape(8, 2048, 1024).astype(np.float32)
